# version 09
# baseline (speedup 1.0000x reference)
import math
from contextlib import ExitStack

import numpy as np
import ml_dtypes

import concourse.bass as bass
import concourse.mybir as mybir
from concourse.bass_utils import run_bass_kernel_spmd

F32 = mybir.dt.float32
BF16 = mybir.dt.bfloat16
I32 = mybir.dt.int32
AF = mybir.ActivationFunctionType
ALU = mybir.AluOpType
AX = mybir.AxisListType.X

D_FULL = 4096
S_FULL = 2048
NCL = 64
TEMP = 0.1
THR = 0.01
EPS = 1e-5
N_CORES = 8
BH = 4
SCALE_A = 1.0 / math.sqrt(128.0)
B_SIG = 4.0

ENGINES = ("gpsimd", "sync", "tensor", "vector", "scalar")


class Plan:
    def __init__(self, nc, ctx):
        self.nc = nc
        self.ctx = ctx
        self.ops = {e: [] for e in ENGINES}
        self.sems = {}

    def sem(self, name):
        h = self.ctx.enter_context(self.nc.semaphore(name))
        self.sems[name] = [h, 0]

    def val(self, name):
        return self.sems[name][1]

    def wait(self, engine, name, value=None):
        h, n = self.sems[name]
        v = n if value is None else value
        if v is None or v <= 0:
            return
        self.ops[engine].append(lambda e, h=h, v=v: e.wait_ge(h, v))

    def mark(self, engine):
        return len(self.ops[engine])

    def op(self, engine, fn, inc=None, k=1, at=None):
        if inc is not None:
            rec = self.sems[inc]
            rec[1] += k
            h = rec[0]
            g = lambda e, fn=fn, h=h, k=k: fn(e).then_inc(h, k)
            if at is None:
                self.ops[engine].append(g)
            else:
                self.ops[engine].insert(at, g)
            return rec[1]
        if at is None:
            self.ops[engine].append(fn)
        else:
            self.ops[engine].insert(at, fn)
        return None

    def dma_at(self, engine, out, in_, sem, at):
        return self.op(engine, lambda e, o=out, i=in_: e.dma_start(out=o, in_=i),
                       sem, 16, at=at)

    def dma(self, engine, out, in_, sem, accum=False):
        if accum:
            return self.op(engine,
                           lambda e, o=out, i=in_: e.dma_start(out=o, in_=i, accum_op=ALU.add),
                           sem, 16)
        return self.op(engine, lambda e, o=out, i=in_: e.dma_start(out=o, in_=i), sem, 16)

    def mm(self, out, lhsT, rhs, start, stop, inc=None):
        return self.op("tensor",
                       lambda e, o=out, l=lhsT, r=rhs, s1=start, s2=stop:
                       e.matmul(o, l, r, start=s1, stop=s2), inc)

    def tp(self, out, in_, ident, inc=None):
        return self.op("tensor",
                       lambda e, o=out, i=in_, d=ident: e.transpose(o, i, d), inc)

    def act(self, out, in_, func, bias=0.0, scale=1.0, accum_out=None, inc="act"):
        return self.op("scalar",
                       lambda e, o=out, i=in_, f=func, b=bias, s=scale, a=accum_out:
                       e.activation(o, i, f, bias=b, scale=s, accum_out=a), inc)

    def copy(self, out, in_, inc="dve"):
        return self.op("vector", lambda e, o=out, i=in_: e.tensor_copy(o, i), inc)

    def tt(self, out, in0, in1, alu, inc="dve"):
        return self.op("vector",
                       lambda e, o=out, a=in0, b=in1, al=alu: e.tensor_tensor(o, a, b, al), inc)

    def ts(self, out, in0, s1, s2, op0, op1=ALU.bypass, inc="dve"):
        return self.op("vector",
                       lambda e, o=out, a=in0, x=s1, y=s2, o0=op0, o1=op1:
                       e.tensor_scalar(o, a, x, y, o0, o1), inc)

    def tss(self, out, in_, scalar, alu, inc="dve"):
        return self.op("vector",
                       lambda e, o=out, a=in_, s=scalar, al=alu:
                       e.tensor_single_scalar(o, a, s, al), inc)

    def stt(self, out, in0, scalar, in1, op0, op1, inc="dve"):
        return self.op("vector",
                       lambda e, o=out, a=in0, s=scalar, b=in1, o0=op0, o1=op1:
                       e.scalar_tensor_tensor(o, a, s, b, o0, o1), inc)

    def red(self, out, in_, alu, inc="dve", absval=None):
        return self.op("vector",
                       lambda e, o=out, i=in_, al=alu, av=absval:
                       e.tensor_reduce(o, i, AX, al, apply_absolute_value=av), inc)

    def recip(self, out, in_, inc="dve"):
        return self.op("vector", lambda e, o=out, i=in_: e.reciprocal(o, i), inc)

    def barrier(self):
        snap = {"pe": self.val("pe"), "dve": self.val("dve"), "act": self.val("act")}
        own = {"tensor": "pe", "vector": "dve", "scalar": "act"}
        for e in ("tensor", "vector", "scalar"):
            for s, v in snap.items():
                if own[e] != s:
                    self.wait(e, s, v)

    def emit(self, block):
        for name in ENGINES:
            ops = self.ops[name]
            if not ops:
                continue

            def body(e, ops=ops):
                for f in ops:
                    f(e)

            getattr(block, name)(body)


def build(D=D_FULL, S=S_FULL):
    NC = D // 128
    HEADS = NC
    NT = S // 128
    BDC = NC // BH
    NQK = 3 * NC
    NS = S // 512
    QH = min(S, 1024)
    NHALF = S // QH
    NQS = QH // 512
    HD_B = D // BH
    B_SCALE = B_SIG / math.sqrt(HD_B)

    nc = bass.Bass("TRN2", target_bir_lowering=False)

    x0 = nc.dram_tensor("x0", [D, S], F32, kind="ExternalInput").ap()
    wqkv = nc.dram_tensor("wqkv", [NQK * 128, NC * 128], F32, kind="ExternalInput").ap()
    wo = nc.dram_tensor("wo", [NC * 128, NC * 128], F32, kind="ExternalInput").ap()
    wffn = nc.dram_tensor("wffn", [NC * 128, NC * 128], F32, kind="ExternalInput").ap()
    cent = nc.dram_tensor("cent", [NC * 128, NCL], F32, kind="ExternalInput").ap()
    bqkv = nc.dram_tensor("bqkv", [128, NQK], F32, kind="ExternalInput").ap()
    bo_in = nc.dram_tensor("bo_in", [128, NC], F32, kind="ExternalInput").ap()
    bffn_in = nc.dram_tensor("bffn_in", [128, NC], F32, kind="ExternalInput").ap()
    asg_in = nc.dram_tensor("asg_in", [128, NC], I32, kind="ExternalInput").ap()
    bvq_in = nc.dram_tensor("bvq_in", [BH, HD_B], F32, kind="ExternalInput").ap()
    bvk_in = nc.dram_tensor("bvk_in", [BH, HD_B], F32, kind="ExternalInput").ap()
    bvv_in = nc.dram_tensor("bvv_in", [BH, HD_B], F32, kind="ExternalInput").ap()
    ident_in = nc.dram_tensor("ident_in", [128, 128], BF16, kind="ExternalInput").ap()
    tri_in = nc.dram_tensor("tri_in", [128, 128], BF16, kind="ExternalInput").ap()
    iota_in = nc.dram_tensor("iota_in", [128, NCL], F32, kind="ExternalInput").ap()
    onec_in = nc.dram_tensor("onec_in", [128, 1], BF16, kind="ExternalInput").ap()
    oner_in = nc.dram_tensor("oner_in", [65, 128], BF16, kind="ExternalInput").ap()
    onef_in = nc.dram_tensor("onef_in", [128, 128], BF16, kind="ExternalInput").ap()

    out = nc.dram_tensor("out", [D, S], F32, kind="ExternalOutput").ap()

    xs = nc.dram_tensor("xs", [D, S], F32, kind="Internal").ap()
    qkvs = nc.dram_tensor("qkvs", [NQK * 128, S], BF16, kind="Internal").ap()
    ctxs = nc.dram_tensor("ctxs", [NC * 128, S], BF16, kind="Internal").ap()
    ybs = nc.dram_tensor("ybs", [NC * 128, S], BF16, kind="Internal").ap()
    cnt_loc = nc.dram_tensor("cnt_loc", [128, NCL], F32, kind="Internal").ap()
    cnt_gl = [nc.dram_tensor(f"cnt_glob{i}", [128, NCL], F32, kind="Internal").ap()
              for i in range(2)]

    with ExitStack() as ctx:
        sb = lambda n, shp, dt: ctx.enter_context(nc.sbuf_tensor(n, shp, dt))
        psumt = lambda n, shp, dt: ctx.enter_context(nc.psum_tensor(n, shp, dt))

        YBW = max(NC * S,
                  10 * S + (S // 128) * min(S, 1024),
                  (NC // BH) * S + (NC // BH) * 512 + (S // 128) * S + 2 * (S // 128) * 128)
        YB = sb("YB", [128, YBW], BF16)
        WT = sb("WT", [128, 2 * NC * 128], BF16)
        EV = sb("EV", [128, 2 * S], F32)
        CAST = sb("CAST", [128, 3 * S], BF16)
        XB = sb("XB", [128, 4 * S], BF16)
        BC = sb("BC", [128, 3 * S], BF16)
        identt = sb("identt", [128, 128], BF16)
        tri = sb("tri", [128, 128], BF16)
        iota_sb = sb("iota_sb", [128, NCL], F32)
        ones_c = sb("ones_c", [128, 1], BF16)
        ones_r = sb("ones_r", [65, 128], BF16)
        ones_f = sb("ones_f", [128, 128], BF16)
        bqkv_sb = sb("bqkv_sb", [128, NQK], F32)
        bo_sb = sb("bo_sb", [128, NC], F32)
        bffn_sb = sb("bffn_sb", [128, NC], F32)
        asg_i_sb = sb("asg_i_sb", [128, NC], I32)
        asg_f_sb = sb("asg_f_sb", [128, NC], F32)
        cq_sb = sb("cq_sb", [128, NC], F32)
        cv_sb = sb("cv_sb", [128, NC], F32)
        rg_sb = sb("rg_sb", [128, NC], F32)
        qgc_sb = sb("qgc_sb", [128, NT], BF16)
        cnt_sb = sb("cnt_sb", [128, NCL], F32)
        cgb_sb = sb("cgb_sb", [128, NCL], F32)
        ew_sb = sb("ew_sb", [128, 2 * NCL], F32)
        CENT = sb("CENT", [128, NC * NCL], BF16)
        hit_sb = BC[:, 2 * S:2 * S + NT * NCL]
        dlT_sb = CAST[:, 2 * S + NT * NCL:2 * S + 2 * NT * NCL]

        PSA = psumt("PSA", [128, 2048], F32)
        PSB = psumt("PSB", [128, 1024], F32)
        PSC = psumt("PSC", [128, 1024], F32)
        PSA16 = PSA[:, :].bitcast(BF16)
        PSB16 = PSB[:, :].bitcast(BF16)
        PSC16 = PSC[:, :].bitcast(BF16)

        p = Plan(nc, ctx)
        for s in ("hld0", "hld1", "hst0", "hst1", "gld0", "gld1",
                  "gst0", "gst1", "gaux", "lnl0", "lnl1", "lnl2", "lnl3", "spl", "pe", "dve", "act", "ccs"):
            p.sem(s)
        block = ctx.enter_context(nc.Block())

        BW = CAST[:, :].bitcast(F32)

        for dst, src in [(identt[:, :], ident_in), (tri[:, :], tri_in),
                         (iota_sb[:, :], iota_in), (ones_c[:, :], onec_in),
                         (ones_r[:, :], oner_in), (ones_f[:, :], onef_in), (bqkv_sb[:, :], bqkv),
                         (bo_sb[:, :], bo_in), (bffn_sb[:, :], bffn_in),
                         (asg_i_sb[:, :], asg_in),
                         (BW[0:BH, 0:HD_B], bvq_in),
                         (BW[0:BH, HD_B:2 * HD_B], bvk_in),
                         (BW[0:BH, 2 * HD_B:3 * HD_B], bvv_in)]:
            p.dma("sync", dst, src, "hld0")
        hld_consts = p.val("hld0")
        p.dma("gpsimd", CENT[:, :].rearrange("p (c n) -> p c n", n=NCL),
              cent.rearrange("(c p) n -> p c n", p=128), "gld0")
        gld_cent = p.val("gld0")
        p.dma("gpsimd", xs, x0, "gaux")
        gaux_xsinit = p.val("gaux")
        p.wait("sync", "hld0", hld_consts)
        p.wait("gpsimd", "gld0", gld_cent)

        p.wait("vector", "hld0", hld_consts)
        p.wait("scalar", "hld0", hld_consts)
        p.op("vector", lambda e: e.memset(WT[:, 0:2 * HD_B], 0.0), "dve")
        p.op("vector", lambda e: e.memset(cnt_sb[:, :], 0.0), "dve")
        alb = EV[0:BH, HD_B:HD_B + 4]
        for bi in range(3):
            w_ap = BW[0:BH, bi * HD_B:(bi + 1) * HD_B]
            p.red(alb[:, bi:bi + 1], w_ap, ALU.add, absval=True)
            p.wait("vector", "dve")
            p.ts(alb[:, bi:bi + 1], alb[:, bi:bi + 1], 1.0 / HD_B, None, ALU.mult)
            sgn = EV[0:BH, 0:HD_B]
            tmp = EV[0:BH, S:S + HD_B]
            p.wait("scalar", "dve", p.val("dve"))
            p.act(sgn, w_ap, AF.Sign)
            p.wait("vector", "act")
            p.tt(tmp, sgn, sgn, ALU.mult)
            p.wait("vector", "dve")
            p.ts(tmp, tmp, -1.0, 1.0, ALU.mult, ALU.add)
            p.wait("vector", "dve")
            p.tt(tmp, sgn, tmp, ALU.add)
            p.wait("vector", "dve")
            p.ts(w_ap, tmp, alb[:, bi:bi + 1], None, ALU.mult)
            p.wait("vector", "dve")
        p.stt(WT[0:BH, 0:HD_B], BW[0:BH, 0:HD_B], B_SCALE,
              BW[0:BH, HD_B:2 * HD_B], ALU.mult, ALU.mult)
        p.copy(WT[0:BH, HD_B:2 * HD_B], BW[0:BH, 2 * HD_B:3 * HD_B])
        p.wait("vector", "dve")
        dve_binds = p.val("dve")
        p.wait("tensor", "dve", dve_binds)
        p.wait("tensor", "hld0", hld_consts)
        for j in range(BDC):
            p.tp(PSB16[:, j * 128:(j + 1) * 128], WT[:, j * 128:(j + 1) * 128], identt[:, :])
        for j in range(BDC):
            p.tp(PSC16[:, j * 128:(j + 1) * 128], WT[:, HD_B + j * 128:HD_B + (j + 1) * 128],
                 identt[:, :], inc="pe" if j == BDC - 1 else None)
        pe_binds = p.val("pe")
        p.wait("vector", "pe", pe_binds)
        for j in range(BDC):
            p.copy(cq_sb[:, j * BH:(j + 1) * BH], PSB16[:, j * 128:j * 128 + BH], inc=None)
            p.copy(cv_sb[:, j * BH:(j + 1) * BH], PSC16[:, j * 128:j * 128 + BH], inc=None)
        p.copy(asg_f_sb[:, :], asg_i_sb[:, :])
        p.wait("vector", "dve")
        prolog_waits = [("dve", p.val("dve")), ("act", p.val("act"))]
        p.barrier()

        wstate = {"k": 0}
        wt_war = {-2: pe_binds, -1: pe_binds}

        def weight_tile_load(src_rows):
            k = wstate["k"]
            slot = k % 2
            dst = WT[:, slot * (NC * 128):(slot + 1) * (NC * 128)]
            p.wait("gpsimd", "pe", wt_war[k - 2])
            v = p.dma("gpsimd", dst, src_rows, "gld0" if slot == 0 else "gld1")
            wstate["k"] += 1
            return dst, v, k

        def qkv_psum_views(buf):
            views = []
            for ns in range(NS):
                if buf == 0:
                    views.append(PSA[:, ns * 512:(ns + 1) * 512])
                else:
                    t = PSB if ns < 2 else PSC
                    views.append(t[:, (ns % 2) * 512:((ns % 2) + 1) * 512])
            return views

        def qkv_psum_spans(buf):
            if buf == 0:
                return [(PSA[:, 0:S], 0, S)]
            spans = [(PSB[:, 0:min(S, 1024)], 0, min(S, 1024))]
            if S > 1024:
                spans.append((PSC[:, 0:S - 1024], 1024, S - 1024))
            return spans

        def layernorm(src_dram, extra_waits=(), chunk_waits=None, hooks=None):
            hld_x, act_sq, pe_done = [], [], []
            for dc in range(NC):
                slot = dc % 4
                lsem = f"lnl{slot}"
                xb = XB[:, slot * S:(slot + 1) * S]
                if dc < 4 and hooks is not None:
                    pos = hooks[dc] + 2 * dc
                    hld_x.append(p.dma_at("gpsimd", xb,
                                          src_dram[dc * 128:(dc + 1) * 128, :],
                                          lsem, at=pos))
                    if chunk_waits is not None:
                        for sname, v in chunk_waits(dc):
                            h2, _ = p.sems[sname]
                            p.ops["gpsimd"].insert(
                                pos, lambda e, h2=h2, v=v: e.wait_ge(h2, v))
                else:
                    if dc >= 4:
                        p.wait("gpsimd", "pe", pe_done[dc - 4])
                        p.wait("gpsimd", "dve", act_sq[dc - 4])
                    if dc == 0:
                        for sname, v in extra_waits:
                            p.wait("gpsimd", sname, v)
                    if chunk_waits is not None:
                        for sname, v in chunk_waits(dc):
                            p.wait("gpsimd", sname, v)
                    hld_x.append(p.dma("gpsimd", xb,
                                       src_dram[dc * 128:(dc + 1) * 128, :], lsem))
                sq = CAST[:, (dc % 2) * S:(dc % 2 + 1) * S]
                p.wait("vector", lsem, hld_x[dc])
                if dc >= 2:
                    p.wait("vector", "pe", pe_done[dc - 2])
                act_sq.append(p.tt(sq, xb, xb, ALU.mult))
                p.wait("tensor", lsem, hld_x[dc])
                p.wait("tensor", "dve", act_sq[dc])
                st, sp = (dc == 0), (dc == NC - 1)
                for ns in range(NS):
                    p.mm(PSA[:, ns * 512:(ns + 1) * 512], ones_f[:, :],
                         xb[:, ns * 512:(ns + 1) * 512], st, sp)
                for ns in range(NS):
                    t = PSB if ns < 2 else PSC
                    p.mm(t[:, (ns % 2) * 512:((ns % 2) + 1) * 512], ones_f[:, :],
                         sq[:, ns * 512:(ns + 1) * 512], st, sp,
                         inc="pe" if ns == NS - 1 else None)
                pe_done.append(p.val("pe"))
            mbc = EV[:, 0:S]
            e2 = EV[:, S:2 * S]
            p.wait("vector", "pe", pe_done[-1])
            p.ts(mbc, PSA[:, 0:S], 1.0 / D, None, ALU.mult)
            if S > 1024:
                p.ts(e2[:, 0:1024], PSB[:, 0:1024], 1.0 / D, None, ALU.mult, inc=None)
                p.ts(EV[:, S + 1024:2 * S], PSC[:, 0:S - 1024], 1.0 / D, None, ALU.mult)
            else:
                p.ts(e2, PSB[:, 0:S], 1.0 / D, None, ALU.mult)
            p.wait("vector", "dve")
            p.tt(PSA[:, 0:S], mbc, mbc, ALU.mult)
            p.wait("vector", "dve")
            p.tt(e2, e2, PSA[:, 0:S], ALU.subtract)
            p.wait("vector", "dve")
            p.ts(e2, e2, EPS, None, ALU.add)
            p.wait("vector", "dve")
            dve_var = p.val("dve")
            p.wait("scalar", "dve", dve_var)
            p.act(PSA[:, 0:S], e2, AF.Ln)
            p.wait("scalar", "act")
            p.act(BC[:, 0:S], PSA[:, 0:S], AF.Exp, scale=-0.5)
            p.wait("vector", "act")
            p.copy(BC[:, S:2 * S], mbc)
            p.wait("vector", "dve")
            return pe_done, act_sq

        def ln_apply(src_dram, ydest_fn, spill=None, stats=None):
            hld_x, dve_y, st_y = [], [], []
            act_entry = p.val("act")
            for dc in range(NC):
                slot = dc % 4
                lsem = f"lnl{slot}"
                if dc < 4 and stats is not None:
                    pe_done, sq_done = stats
                    p.wait("gpsimd", "pe", pe_done[NC - 4 + dc])
                    p.wait("gpsimd", "dve", sq_done[NC - 4 + dc])
                if dc >= 4:
                    p.wait("gpsimd", "dve", dve_y[dc - 4])
                xb = XB[:, slot * S:(slot + 1) * S]
                hld_x.append(p.dma("gpsimd", xb,
                                   src_dram[dc * 128:(dc + 1) * 128, :], lsem))
                t16 = CAST[:, (dc % 2) * S:(dc % 2 + 1) * S]
                dst = ydest_fn(dc)
                p.wait("vector", lsem, hld_x[dc])
                if dc == 0:
                    p.wait("vector", "act", act_entry)
                p.tt(t16, xb, BC[:, S:2 * S], ALU.subtract)
                p.wait("vector", "dve")
                dve_y.append(p.tt(dst, t16, BC[:, 0:S], ALU.mult))
                if spill is not None:
                    p.wait("sync", "dve", dve_y[dc])
                    st_y.append(spill(dc, dst))
            return dve_y, st_y

        def dense_phase(w_src, n_fc, rhs_fn, evac_fn, first_waits=()):
            tiles = [weight_tile_load(w_src[0:128, :])]
            if n_fc > 1:
                tiles.append(weight_tile_load(w_src[128:256, :]))
            evac_dve = []
            for fc in range(n_fc):
                dst, gv, k = tiles[fc]
                buf = k % 2
                views = qkv_psum_views(buf)
                p.wait("tensor", "gld0" if buf == 0 else "gld1", gv)
                if fc < 2:
                    for sname, v in first_waits:
                        p.wait("tensor", sname, v)
                if fc >= 2:
                    p.wait("tensor", "dve", evac_dve[fc - 2])
                for dc in range(NC):
                    lhsT = dst[:, dc * 128:(dc + 1) * 128]
                    st, sp = (dc == 0), (dc == NC - 1)
                    for ns in range(NS):
                        p.mm(views[ns], lhsT, rhs_fn(dc)[:, ns * 512:(ns + 1) * 512],
                             st, sp, inc="pe" if (sp and ns == NS - 1) else None)
                wt_war[k] = p.val("pe")
                if fc + 2 < n_fc:
                    tiles.append(weight_tile_load(w_src[(fc + 2) * 128:(fc + 3) * 128, :]))
                evac_dve.append(evac_fn(fc, qkv_psum_spans(buf), p.val("pe")))

        lnstats1 = layernorm(x0, extra_waits=prolog_waits)
        ln_apply(x0, lambda dc: YB[:, dc * S:(dc + 1) * S], stats=lnstats1)
        p.barrier()

        a3_hst = {0: [], 1: []}

        def a3_evac(fc, spans, pe_cnt):
            slot = fc % 2
            hsem = "hst0" if slot == 0 else "hst1"
            tdst = CAST[:, slot * S:(slot + 1) * S]
            p.wait("vector", "pe", pe_cnt)
            if a3_hst[slot]:
                p.wait("vector", hsem, a3_hst[slot][-1])
            for (pv, c0, w) in spans:
                p.ts(tdst[:, c0:c0 + w], pv, bqkv_sb[:, fc:fc + 1], None, ALU.add,
                     inc="dve" if c0 + w == S else None)
            dve_ev = p.val("dve")
            p.wait("sync", "dve", dve_ev)
            a3_hst[slot].append(
                p.dma("sync", qkvs[fc * 128:(fc + 1) * 128, :], tdst, hsem))
            return dve_ev

        dense_phase(wqkv, NQK, lambda dc: YB[:, dc * S:(dc + 1) * S], a3_evac)
        a3_stores = [("hst0", p.val("hst0")), ("hst1", p.val("hst1"))]
        p.barrier()

        EOFF = 10 * S
        a4_hld = []
        a4_head_pe = []
        a4_ctx_st = []

        def a4_load(h):
            slot = h % 2
            hsem = "hld0" if slot == 0 else "hld1"
            st = slot * 5 * S
            if h >= 2:
                p.wait("sync", "pe", a4_head_pe[h - 2])
            if h == 0:
                for sname, v in a3_stores:
                    p.wait("sync", sname, v)
            v = None
            for i, base in enumerate((0, NC, 2 * NC)):
                v = p.dma("sync", YB[:, st + i * S:st + (i + 1) * S],
                          qkvs[(base + h) * 128:(base + h + 1) * 128, :], hsem)
            return v

        a4_hld.append(a4_load(0))
        a4_state = {"prev_ctx": None, "prev_rr": None, "pend": None}
        act_hist = []

        def a4_tail():
            pend = a4_state["pend"]
            if pend is None:
                return
            h_, hq_, cx_, q0_, pe_ctx = pend
            p.wait("scalar", "pe", pe_ctx)
            if a4_state["prev_ctx"] is not None:
                p.wait("scalar", "dve", a4_state["prev_ctx"])
            p.act(EV[:, 0:QH], PSA[:, 0:QH], AF.Ln)
            a4_state["prev_rr"] = p.val("act")
            p.wait("scalar", "act")
            p.act(CAST[:, 0:QH], EV[:, 0:QH], AF.Exp, scale=-1.0)
            act_rr = p.val("act")
            p.wait("vector", "act", act_rr)
            if h_ >= 2 and hq_ == 0:
                p.wait("vector", "hst0" if h_ % 2 == 0 else "hst1", a4_ctx_st[h_ - 2])
            p.tt(cx_[:, q0_:q0_ + QH], PSA[:, 1024:1024 + QH], CAST[:, 0:QH], ALU.mult)
            p.wait("vector", "dve")
            a4_state["prev_ctx"] = p.val("dve")
            if hq_ == NHALF - 1:
                a4_head_pe.append(pe_ctx)
                p.wait("sync", "dve", a4_state["prev_ctx"])
                a4_ctx_st.append(p.dma("sync", ctxs[h_ * 128:(h_ + 1) * 128, :], cx_,
                                       "hst0" if h_ % 2 == 0 else "hst1"))
            a4_state["pend"] = None

        for h in range(HEADS):
            slot = h % 2
            hsem = "hld0" if slot == 0 else "hld1"
            st = slot * 5 * S
            qv = YB[:, st:st + S]
            kv = YB[:, st + S:st + 2 * S]
            vf = YB[:, st + 2 * S:st + 3 * S]
            vt = YB[:, st + 3 * S:st + 4 * S]
            cx = YB[:, st + 4 * S:st + 5 * S]
            p.wait("tensor", hsem, a4_hld[h])
            if act_hist:
                p.wait("tensor", "act", act_hist[-1])
            for tt in range(NT):
                p.tp(PSB16[:, tt * 128:(tt + 1) * 128],
                     vf[:, tt * 128:(tt + 1) * 128], identt[:, :],
                     inc="pe" if tt == NT - 1 else None)
            p.wait("vector", "pe", p.val("pe"))
            p.copy(vt[:, 0:NT * 128], PSB16[:, 0:NT * 128])
            p.wait("vector", "dve")
            dve_vt = p.val("dve")

            for hq in range(NHALF):
                q0 = hq * QH
                act_cur = []

                def emit_scores(sc, q0=q0, qv=qv, kv=kv, act_cur=act_cur,
                                dve_vt=dve_vt, hq=hq):
                    sps = PSB if sc % 2 == 0 else PSC
                    if len(act_hist) >= 2:
                        p.wait("tensor", "act", act_hist[-2])
                    if sc == 0 and hq == 0:
                        p.wait("tensor", "dve", dve_vt)
                    for ns in range(NQS):
                        p.mm(sps[:, ns * 512:(ns + 1) * 512],
                             kv[:, sc * 128:(sc + 1) * 128],
                             qv[:, q0 + ns * 512:q0 + (ns + 1) * 512],
                             True, True, inc="pe" if ns == NQS - 1 else None)
                    p.wait("scalar", "pe", p.val("pe"))
                    eb = YB[:, EOFF + sc * QH:EOFF + (sc + 1) * QH]
                    v = p.act(eb, sps[:, 0:QH], AF.Exp, scale=SCALE_A)
                    act_cur.append(v)
                    act_hist.append(v)

                a4_tail()
                emit_scores(0)
                if NT > 1:
                    emit_scores(1)
                if hq == 0 and h + 1 < HEADS:
                    a4_hld.append(a4_load(h + 1))
                for sc in range(NT):
                    p.wait("tensor", "act", act_cur[sc])
                    if sc == 0 and a4_state["prev_rr"] is not None:
                        p.wait("tensor", "act", a4_state["prev_rr"])
                    eb = YB[:, EOFF + sc * QH:EOFF + (sc + 1) * QH]
                    st_, sp_ = (sc == 0), (sc == NT - 1)
                    for ns in range(NQS):
                        p.mm(PSA[:, ns * 512:(ns + 1) * 512], ones_f[:, :],
                             eb[:, ns * 512:(ns + 1) * 512], st_, sp_)
                    if sc == 0 and a4_state["prev_ctx"] is not None:
                        p.wait("tensor", "dve", a4_state["prev_ctx"])
                    for ns in range(NQS):
                        p.mm(PSA[:, 1024 + ns * 512:1024 + (ns + 1) * 512],
                             vt[:, sc * 128:(sc + 1) * 128],
                             eb[:, ns * 512:(ns + 1) * 512], st_, sp_,
                             inc="pe" if ns == NQS - 1 else None)
                    if sc + 2 < NT:
                        emit_scores(sc + 2)
                a4_state["pend"] = (h, hq, cx, q0, p.val("pe"))
        a4_tail()
        a4_stores = [("hst0", p.val("hst0")), ("hst1", p.val("hst1"))]
        p.barrier()
        a5_acc_rec = []
        a5_hooks = []

        for sname, v in a4_stores:
            p.wait("sync", sname, v)
        p.wait("sync", "pe", a4_head_pe[-1])
        p.wait("sync", "dve", p.val("dve"))
        ctx_ld = None
        for dc in range(NC):
            ctx_ld = p.dma("sync", YB[:, dc * S:(dc + 1) * S],
                           ctxs[dc * 128:(dc + 1) * 128, :], "hld0")
        p.wait("tensor", "hld0", ctx_ld)
        a5_gst = {0: [], 1: []}

        def a5_evac(fc, spans, pe_cnt):
            slot = fc % 2
            gsem = "gst0" if slot == 0 else "gst1"
            tdst = EV[:, slot * S:(slot + 1) * S]
            p.wait("vector", "pe", pe_cnt)
            if a5_gst[slot]:
                p.wait("vector", gsem, a5_gst[slot][-1])
            for (pv, c0, w) in spans:
                p.ts(tdst[:, c0:c0 + w], pv, bo_sb[:, fc:fc + 1], None, ALU.add,
                     inc="dve" if c0 + w == S else None)
            dve_ev = p.val("dve")
            p.wait("gpsimd", "dve", dve_ev)
            if fc == 0:
                p.wait("gpsimd", "gaux", gaux_xsinit)
            a5_gst[slot].append(
                p.dma("gpsimd", xs[fc * 128:(fc + 1) * 128, :], tdst, gsem, accum=True))
            a5_acc_rec.append((gsem, a5_gst[slot][-1]))
            if fc < 4:
                a5_hooks.append(p.mark("gpsimd"))
            return dve_ev

        dense_phase(wo, NC, lambda dc: YB[:, dc * S:(dc + 1) * S], a5_evac)
        a5_accums = [("gst0", p.val("gst0")), ("gst1", p.val("gst1"))]
        p.barrier()

        def ffn_phase(call_idx, accum_target, acc_rec, copy_xs_to_out, hooks=None):
            lnstats = layernorm(xs, chunk_waits=lambda dc: [acc_rec[dc]], hooks=hooks)
            dve_y, _ = ln_apply(xs, lambda dc: YB[:, dc * S:(dc + 1) * S],
                                stats=lnstats)
            p.wait("tensor", "gld0", gld_cent)
            for dc in range(NC):
                st, sp = (dc == 0), (dc == NC - 1)
                p.wait("tensor", "dve", dve_y[dc])
                for ns in range(NS):
                    p.mm(PSA[0:NCL, ns * 512:(ns + 1) * 512],
                         CENT[:, dc * NCL:(dc + 1) * NCL],
                         YB[:, dc * S + ns * 512:dc * S + (ns + 1) * 512],
                         st, sp, inc="pe" if (sp and ns == NS - 1) else None)
            pe_dl = p.val("pe")
            dl16 = CAST[0:NCL, 0:S]
            p.wait("vector", "pe", pe_dl)
            p.copy(dl16, PSA[0:NCL, 0:S])
            p.wait("vector", "dve")
            dve_dl = p.val("dve")
            p.wait("tensor", "dve", dve_dl)
            for tt in range(NT):
                p.tp(PSB16[:, tt * NCL:(tt + 1) * NCL], dl16[:, tt * 128:(tt + 1) * 128],
                     identt[0:NCL, 0:NCL], inc="pe" if tt == NT - 1 else None)
            p.wait("vector", "pe", p.val("pe"))
            p.copy(dlT_sb[:, 0:NT * NCL], PSB16[:, 0:NT * NCL])
            p.wait("vector", "dve")
            dlt_all = dlT_sb[:, 0:NT * NCL]
            d3 = lambda ap: ap.rearrange("p (g c) -> p g c", c=NCL)
            col3 = lambda ap: ap.rearrange("p (g o) -> p g o", o=1)
            mx = EV[:, 3 * 512:3 * 512 + NT]
            Ssum = EV[:, 3 * 512 + NT:3 * 512 + 2 * NT]
            thr = EV[:, 3 * 512 + 2 * NT:3 * 512 + 3 * NT]
            qm = EV[:, 3 * 512 + 3 * NT:3 * 512 + 4 * NT]
            dtile = CAST[:, 2 * S:2 * S + NT * NCL]
            etile = EV[:, 0:NT * NCL]
            util = EV[:, S:S + NT * NCL]
            p.red(col3(mx), d3(dlt_all), ALU.max)
            p.wait("vector", "dve")
            p.tt(d3(dtile), d3(dlt_all), col3(mx).broadcast_to((128, NT, NCL)),
                 ALU.subtract)
            p.wait("vector", "dve")
            p.wait("scalar", "dve", p.val("dve"))
            p.act(etile, dtile, AF.Exp, scale=1.0 / TEMP)
            p.wait("vector", "act")
            p.red(col3(Ssum), d3(etile), ALU.add)
            p.wait("vector", "dve")
            p.ts(thr, Ssum, THR, None, ALU.mult)
            p.wait("vector", "dve")
            p.tt(d3(util), d3(etile), col3(thr).broadcast_to((128, NT, NCL)),
                 ALU.subtract)
            p.wait("vector", "dve")
            p.tss(hit_sb, util, 0.0, ALU.is_gt)
            p.red(col3(qm), d3(util), ALU.max)
            p.wait("vector", "dve")
            p.tss(qgc_sb[:, 0:NT], qm, 0.0, ALU.is_gt)
            p.wait("vector", "dve")
            dve_hits = p.val("dve")
            p.wait("tensor", "dve", dve_hits)
            for ns in range(NT * NCL // 512):
                p.mm(PSC[:, ns * 512:(ns + 1) * 512], ones_f[:, :],
                     hit_sb[:, ns * 512:(ns + 1) * 512], True, True,
                     inc="pe" if ns == NT * NCL // 512 - 1 else None)
            p.wait("vector", "pe", p.val("pe"))
            p.red(cnt_sb[:, :].rearrange("p (c o) -> p c o", o=1),
                  PSC[:, 0:NT * NCL].rearrange("p (g c) -> p g c", c=NCL).transpose([0, 2, 1]),
                  ALU.add)
            p.wait("vector", "dve")
            dve_cnt = p.val("dve")
            p.wait("gpsimd", "gaux", p.val("gaux"))
            p.wait("gpsimd", "dve", dve_cnt)
            p.dma("gpsimd", cnt_loc, cnt_sb[:, :], "gaux")
            p.wait("gpsimd", "gaux", p.val("gaux"))
            p.op("gpsimd", lambda e, o=cnt_gl[call_idx]: e.collective_compute(
                "AllReduce", ALU.add, replica_groups=[list(range(N_CORES))],
                ins=[cnt_loc.opt()], outs=[o.opt()]), "ccs", 1)
            p.wait("gpsimd", "ccs", p.val("ccs"))
            p.dma("gpsimd", cgb_sb[:, :], cnt_gl[call_idx], "gaux")
            gaux_cntg = p.val("gaux")
            p.wait("tensor", "dve", dve_cnt)
            for tt in range(NT):
                p.tp(PSB16[0:1, tt * 128:(tt + 1) * 128], qgc_sb[:, tt:tt + 1],
                     identt[:, :], inc="pe" if tt == NT - 1 else None)
            qg16 = CAST[0:1, 2 * S:3 * S]
            p.wait("vector", "pe", p.val("pe"))
            for tt in range(NT):
                p.copy(qg16[:, tt * 128:(tt + 1) * 128],
                       PSB16[0:1, tt * 128:(tt + 1) * 128],
                       inc="dve" if tt == NT - 1 else None)
            p.wait("vector", "dve")
            dve_qgrow = p.val("dve")
            p.wait("tensor", "dve", dve_qgrow)
            for ns in range(NS):
                p.mm(PSA[:, ns * 512:(ns + 1) * 512], ones_r[0:1, :],
                     qg16[:, ns * 512:(ns + 1) * 512], True, True,
                     inc="pe" if ns == NS - 1 else None)
            p.wait("vector", "pe", p.val("pe"))
            p.copy(BC[:, 2 * S:3 * S], PSA[:, 0:S])
            p.wait("vector", "dve")
            dve_qgbc = p.val("dve")
            p.wait("vector", "gaux", gaux_cntg)
            dve_psfree = dve_qgbc
            for fc in range(NC):
                p.ts(ew_sb[:, 0:NCL], iota_sb[:, :], asg_f_sb[:, fc:fc + 1], None,
                     ALU.subtract)
                p.wait("vector", "dve")
                p.tss(ew_sb[:, NCL:2 * NCL], ew_sb[:, 0:NCL], 0.0, ALU.is_equal)
                p.wait("vector", "dve")
                p.tt(ew_sb[:, 0:NCL], ew_sb[:, NCL:2 * NCL], cgb_sb[:, :], ALU.mult)
                p.wait("vector", "dve")
                p.red(EV[:, 0:1], ew_sb[:, 0:NCL], ALU.add)
                p.wait("vector", "dve")
                p.tss(rg_sb[:, fc:fc + 1], EV[:, 0:1], 0.0, ALU.is_gt)
                p.wait("vector", "dve")
            dve_gates = p.val("dve")
            copy_done = None
            if copy_xs_to_out:
                for sname, v in (acc_rec[-1], acc_rec[-2]):
                    p.wait("gpsimd", sname, v)
                p.wait("gpsimd", "gaux", p.val("gaux"))
                p.dma("gpsimd", out, xs, "gaux")
                copy_done = p.val("gaux")
            f_gst = {0: [], 1: []}
            f_hooks = []

            def f_evac(fc, spans, pe_cnt2):
                slot = fc % 2
                gsem = "gst0" if slot == 0 else "gst1"
                t2 = EV[:, slot * S:(slot + 1) * S]
                p.wait("vector", "pe", pe_cnt2)
                p.wait("vector", "dve", dve_gates)
                if f_gst[slot]:
                    p.wait("vector", gsem, f_gst[slot][-1])
                for (pv, c0, w) in spans:
                    p.ts(t2[:, c0:c0 + w], pv, bffn_sb[:, fc:fc + 1],
                         rg_sb[:, fc:fc + 1], ALU.add, ALU.mult,
                         inc="dve" if c0 + w == S else None)
                dve_ev = p.val("dve")
                p.wait("vector", "dve")
                p.tt(t2, t2, BC[:, 2 * S:3 * S], ALU.mult)
                p.wait("gpsimd", "dve", p.val("dve"))
                if fc == 0 and copy_done is not None:
                    p.wait("gpsimd", "gaux", copy_done)
                f_gst[slot].append(
                    p.dma("gpsimd", accum_target[fc * 128:(fc + 1) * 128, :],
                          t2, gsem, accum=True))
                f_acc_rec.append((gsem, f_gst[slot][-1]))
                if fc < 4:
                    f_hooks.append(p.mark("gpsimd"))
                return dve_ev

            f_acc_rec = []
            dense_phase(wffn, NC, lambda dc: YB[:, dc * S:(dc + 1) * S], f_evac,
                        first_waits=[("dve", dve_psfree)])
            f_hooks_out[0] = f_hooks
            return f_acc_rec

        f_hooks_out = [None]
        ffn1_acc_rec = ffn_phase(0, xs, a5_acc_rec, copy_xs_to_out=False, hooks=a5_hooks)
        p.barrier()

        lnstats_b = layernorm(xs, chunk_waits=lambda dc: [ffn1_acc_rec[dc]],
                              hooks=f_hooks_out[0])
        _, yb_st = ln_apply(xs, lambda dc: YB[:, dc * S:(dc + 1) * S],
                            spill=lambda dc, dst: p.dma(
                                "sync", ybs[dc * 128:(dc + 1) * 128, :], dst, "spl"),
                            stats=lnstats_b)
        p.barrier()

        QP0 = BDC * S
        A0 = QP0 + BDC * 512
        Y0 = A0 + NT * S
        NQ = S // 512
        b_gst = {0: [], 1: []}
        b_head_pe = None
        b_head_dve = None
        b_acc_rec = []
        b_hooks = []
        for h in range(BH):
            p.wait("sync", "spl", yb_st[-1])
            if b_head_pe is not None:
                p.wait("sync", "pe", b_head_pe)
                p.wait("sync", "dve", b_head_dve)
            yh_ld = p.dma("sync", YB[:, 0:BDC * S].rearrange("p (c n) -> p c n", n=S),
                          ybs[h * BDC * 128:(h + 1) * BDC * 128, :].rearrange(
                              "(c p) n -> p c n", p=128), "hld0")
            p.wait("vector", "hld0", yh_ld)
            p.wait("tensor", "hld0", yh_ld)
            sc_idx = 0
            act_sig = []
            pe_prev_tq = None
            for tq in range(NQ):
                t0 = tq * 512
                if pe_prev_tq is not None:
                    p.wait("vector", "pe", pe_prev_tq)
                for dc in range(BDC):
                    p.ts(YB[:, QP0 + dc * 512:QP0 + (dc + 1) * 512],
                         YB[:, dc * S + t0:dc * S + t0 + 512],
                         cq_sb[:, dc * BH + h:dc * BH + h + 1], None, ALU.mult,
                         inc="dve" if dc == BDC - 1 else None)
                dve_qp = p.val("dve")
                p.wait("tensor", "dve", dve_qp)
                smax = (t0 + 512) // 128
                for sc in range(smax):
                    c0 = max(t0, sc * 128)
                    w = t0 + 512 - c0
                    sps = PSB[:, 0:512] if sc_idx % 2 == 0 else PSB[:, 512:1024]
                    if sc_idx >= 2:
                        p.wait("tensor", "act", act_sig[sc_idx - 2])
                    for dc in range(BDC):
                        p.mm(sps[:, 0:w],
                             YB[:, dc * S + sc * 128:dc * S + (sc + 1) * 128],
                             YB[:, QP0 + dc * 512 + (c0 - t0):QP0 + (dc + 1) * 512],
                             dc == 0, dc == BDC - 1,
                             inc="pe" if dc == BDC - 1 else None)
                    p.wait("scalar", "pe", p.val("pe"))
                    adst = YB[:, A0 + sc * S + c0:A0 + sc * S + t0 + 512]
                    act_sig.append(p.act(adst, sps[:, 0:w], AF.Sigmoid))
                    if c0 == sc * 128:
                        p.wait("vector", "act", act_sig[-1])
                        p.tt(YB[:, A0 + sc * S + c0:A0 + sc * S + c0 + 128],
                             YB[:, A0 + sc * S + c0:A0 + sc * S + c0 + 128],
                             tri[:, :], ALU.mult)
                    sc_idx += 1
                pe_prev_tq = p.val("pe")
            dve_attn = p.val("dve")
            act_attn = p.val("act")
            dve_ytcp = []

            def b_transposes(dc):
                psT = PSC16 if dc % 2 == 0 else PSB16
                p.wait("tensor", "dve", dve_attn)
                p.wait("tensor", "act", act_attn)
                if dc >= 2:
                    p.wait("tensor", "dve", dve_ytcp[dc - 2])
                for sc in range(NT):
                    p.tp(psT[:, sc * 128:(sc + 1) * 128],
                         YB[:, dc * S + sc * 128:dc * S + (sc + 1) * 128],
                         identt[:, :], inc="pe" if sc == NT - 1 else None)
                p.wait("vector", "pe", p.val("pe"))
                p.copy(YB[:, Y0 + (dc % 2) * NT * 128:Y0 + (dc % 2 + 1) * NT * 128],
                       psT[:, 0:NT * 128])
                p.wait("vector", "dve")
                dve_ytcp.append(p.val("dve"))

            b_transposes(0)
            b_evac_dve = []
            for dc in range(BDC):
                if dc + 1 < BDC:
                    b_transposes(dc + 1)
                yt0 = Y0 + (dc % 2) * NT * 128
                p.wait("tensor", "dve", dve_ytcp[dc])
                if dc >= 1:
                    p.wait("tensor", "dve", b_evac_dve[dc - 1])
                for sc in range(NT):
                    c = sc * 128
                    while c < S:
                        w = min(512 - (c % 512), S - c)
                        j = c // 512
                        sp = (sc == min(NT - 1, 4 * j + 3))
                        p.mm(PSA[:, c:c + w], YB[:, yt0 + sc * 128:yt0 + (sc + 1) * 128],
                             YB[:, A0 + sc * S + c:A0 + sc * S + c + w],
                             sc == 0, sp,
                             inc="pe" if (sc == NT - 1 and c + w == S) else None)
                        c += w
                pe_ctx = p.val("pe")
                gdc = h * BDC + dc
                slot = gdc % 2
                gsem = "gst0" if slot == 0 else "gst1"
                t2 = EV[:, slot * S:(slot + 1) * S]
                p.wait("vector", "pe", pe_ctx)
                if b_gst[slot]:
                    p.wait("vector", gsem, b_gst[slot][-1])
                p.ts(t2, PSA[:, 0:S], cv_sb[:, dc * BH + h:dc * BH + h + 1], None,
                     ALU.mult)
                b_evac_dve.append(p.val("dve"))
                p.wait("gpsimd", "dve", p.val("dve"))
                b_gst[slot].append(
                    p.dma("gpsimd", xs[gdc * 128:(gdc + 1) * 128, :], t2, gsem,
                          accum=True))
                b_acc_rec.append((gsem, b_gst[slot][-1]))
                if gdc < 4:
                    b_hooks.append(p.mark("gpsimd"))
            b_head_pe = p.val("pe")
            b_head_dve = p.val("dve")
        p.barrier()

        ffn_phase(1, out, b_acc_rec, copy_xs_to_out=True, hooks=b_hooks)

        for sname in ("gst0", "gst1", "gaux", "hst0", "hst1"):
            p.wait("sync", sname, p.val(sname))
            p.wait("gpsimd", sname, p.val(sname))

        p.emit(block)

    return nc


def _prep_inputs(L0_attn_a_x, L0_attn_a_attn_qkvw, L0_attn_a_attn_qkvb,
                 L0_attn_a_attn_ow, L0_attn_a_attn_ob, L0_attn_b_bv_q,
                 L0_attn_b_bv_k, L0_attn_b_bv_v, L0_ffn_weight, L0_ffn_bias,
                 L0_ffn_centroids, L0_ffn_assignments, D, S):
    NC = D // 128
    NQK = 3 * NC
    f32 = np.float32
    wqkv_a = np.ascontiguousarray(L0_attn_a_attn_qkvw, dtype=f32)
    wo_a = np.ascontiguousarray(L0_attn_a_attn_ow, dtype=f32)
    wffn_t = np.ascontiguousarray(np.asarray(L0_ffn_weight, dtype=f32).T)

    def blk(w, nfc):
        return np.ascontiguousarray(
            w.reshape(NC, 128, nfc, 128).transpose(2, 1, 0, 3).reshape(nfc * 128, NC * 128))

    shared = {
        "wqkv": blk(wqkv_a, NQK),
        "wo": blk(wo_a, NC),
        "wffn": blk(wffn_t, NC),
        "cent": np.ascontiguousarray(np.asarray(L0_ffn_centroids, dtype=f32).T),
        "bqkv": np.ascontiguousarray(np.asarray(L0_attn_a_attn_qkvb, f32).reshape(NQK, 128).T),
        "bo_in": np.ascontiguousarray(np.asarray(L0_attn_a_attn_ob, f32).reshape(NC, 128).T),
        "bffn_in": np.ascontiguousarray(np.asarray(L0_ffn_bias, f32).reshape(NC, 128).T),
        "asg_in": np.ascontiguousarray(
            np.asarray(L0_ffn_assignments, np.int32).reshape(NC, 128).T),
        "bvq_in": np.ascontiguousarray(L0_attn_b_bv_q, dtype=f32),
        "bvk_in": np.ascontiguousarray(L0_attn_b_bv_k, dtype=f32),
        "bvv_in": np.ascontiguousarray(L0_attn_b_bv_v, dtype=f32),
        "ident_in": np.eye(128, dtype=ml_dtypes.bfloat16),
        "tri_in": np.triu(np.ones((128, 128))).astype(ml_dtypes.bfloat16),
        "iota_in": np.broadcast_to(np.arange(NCL, dtype=f32), (128, NCL)).copy(),
        "onec_in": np.ones((128, 1), dtype=ml_dtypes.bfloat16),
        "oner_in": np.ones((65, 128), dtype=ml_dtypes.bfloat16),
        "onef_in": np.ones((128, 128), dtype=ml_dtypes.bfloat16),
    }
    x = np.asarray(L0_attn_a_x, dtype=f32)
    in_maps = []
    for b in range(N_CORES):
        m = dict(shared)
        m["x0"] = np.ascontiguousarray(x[b].T)
        in_maps.append(m)
    return in_maps


def kernel(**inputs):
    D, S = D_FULL, S_FULL
    in_maps = _prep_inputs(D=D, S=S, **inputs)
    nc = build(D, S)
    res = run_bass_kernel_spmd(nc, in_maps, core_ids=list(range(N_CORES)))
    outs = [np.ascontiguousarray(np.asarray(res.results[b]["out"]).T)
            for b in range(N_CORES)]
    return np.stack(outs, axis=0)


# version 10 (final)
# speedup vs baseline: 1.0572x; 1.0572x over previous
import math
from contextlib import ExitStack

import numpy as np
import ml_dtypes

import concourse.bass as bass
import concourse.mybir as mybir
from concourse.bass_utils import run_bass_kernel_spmd

F32 = mybir.dt.float32
BF16 = mybir.dt.bfloat16
I32 = mybir.dt.int32
AF = mybir.ActivationFunctionType
ALU = mybir.AluOpType
AX = mybir.AxisListType.X

D_FULL = 4096
S_FULL = 2048
NCL = 64
TEMP = 0.1
THR = 0.01
EPS = 1e-5
N_CORES = 8
BH = 4
SCALE_A = 1.0 / math.sqrt(128.0)
B_SIG = 4.0

ENGINES = ("gpsimd", "sync", "tensor", "vector", "scalar")


class Plan:
    def __init__(self, nc, ctx):
        self.nc = nc
        self.ctx = ctx
        self.ops = {e: [] for e in ENGINES}
        self.sems = {}

    def sem(self, name):
        h = self.ctx.enter_context(self.nc.semaphore(name))
        self.sems[name] = [h, 0]

    def val(self, name):
        return self.sems[name][1]

    def wait(self, engine, name, value=None):
        h, n = self.sems[name]
        v = n if value is None else value
        if v is None or v <= 0:
            return
        self.ops[engine].append(lambda e, h=h, v=v: e.wait_ge(h, v))

    def mark(self, engine):
        return len(self.ops[engine])

    def op(self, engine, fn, inc=None, k=1, at=None):
        if inc is not None:
            rec = self.sems[inc]
            rec[1] += k
            h = rec[0]
            g = lambda e, fn=fn, h=h, k=k: fn(e).then_inc(h, k)
            if at is None:
                self.ops[engine].append(g)
            else:
                self.ops[engine].insert(at, g)
            return rec[1]
        if at is None:
            self.ops[engine].append(fn)
        else:
            self.ops[engine].insert(at, fn)
        return None

    def dma_at(self, engine, out, in_, sem, at):
        return self.op(engine, lambda e, o=out, i=in_: e.dma_start(out=o, in_=i),
                       sem, 16, at=at)

    def dma(self, engine, out, in_, sem, accum=False):
        if accum:
            return self.op(engine,
                           lambda e, o=out, i=in_: e.dma_start(out=o, in_=i, accum_op=ALU.add),
                           sem, 16)
        return self.op(engine, lambda e, o=out, i=in_: e.dma_start(out=o, in_=i), sem, 16)

    def mm(self, out, lhsT, rhs, start, stop, inc=None):
        return self.op("tensor",
                       lambda e, o=out, l=lhsT, r=rhs, s1=start, s2=stop:
                       e.matmul(o, l, r, start=s1, stop=s2), inc)

    def tp(self, out, in_, ident, inc=None):
        return self.op("tensor",
                       lambda e, o=out, i=in_, d=ident: e.transpose(o, i, d), inc)

    def act(self, out, in_, func, bias=0.0, scale=1.0, accum_out=None, inc="act"):
        return self.op("scalar",
                       lambda e, o=out, i=in_, f=func, b=bias, s=scale, a=accum_out:
                       e.activation(o, i, f, bias=b, scale=s, accum_out=a), inc)

    def copy(self, out, in_, inc="dve"):
        return self.op("vector", lambda e, o=out, i=in_: e.tensor_copy(o, i), inc)

    def tt(self, out, in0, in1, alu, inc="dve"):
        return self.op("vector",
                       lambda e, o=out, a=in0, b=in1, al=alu: e.tensor_tensor(o, a, b, al), inc)

    def ts(self, out, in0, s1, s2, op0, op1=ALU.bypass, inc="dve"):
        return self.op("vector",
                       lambda e, o=out, a=in0, x=s1, y=s2, o0=op0, o1=op1:
                       e.tensor_scalar(o, a, x, y, o0, o1), inc)

    def tss(self, out, in_, scalar, alu, inc="dve"):
        return self.op("vector",
                       lambda e, o=out, a=in_, s=scalar, al=alu:
                       e.tensor_single_scalar(o, a, s, al), inc)

    def stt(self, out, in0, scalar, in1, op0, op1, inc="dve"):
        return self.op("vector",
                       lambda e, o=out, a=in0, s=scalar, b=in1, o0=op0, o1=op1:
                       e.scalar_tensor_tensor(o, a, s, b, o0, o1), inc)

    def red(self, out, in_, alu, inc="dve", absval=None):
        return self.op("vector",
                       lambda e, o=out, i=in_, al=alu, av=absval:
                       e.tensor_reduce(o, i, AX, al, apply_absolute_value=av), inc)

    def recip(self, out, in_, inc="dve"):
        return self.op("vector", lambda e, o=out, i=in_: e.reciprocal(o, i), inc)

    def barrier(self):
        snap = {"pe": self.val("pe"), "dve": self.val("dve"), "act": self.val("act")}
        own = {"tensor": "pe", "vector": "dve", "scalar": "act"}
        for e in ("tensor", "vector", "scalar"):
            for s, v in snap.items():
                if own[e] != s:
                    self.wait(e, s, v)

    def emit(self, block):
        for name in ENGINES:
            ops = self.ops[name]
            if not ops:
                continue

            def body(e, ops=ops):
                for f in ops:
                    f(e)

            getattr(block, name)(body)


def build(D=D_FULL, S=S_FULL):
    NC = D // 128
    HEADS = NC
    NT = S // 128
    BDC = NC // BH
    NQK = 3 * NC
    NS = S // 512
    QH = min(S, 1024)
    NHALF = S // QH
    NQS = QH // 512
    HD_B = D // BH
    B_SCALE = B_SIG / math.sqrt(HD_B)

    nc = bass.Bass("TRN2", target_bir_lowering=False)

    x0 = nc.dram_tensor("x0", [D, S], F32, kind="ExternalInput").ap()
    wqkv = nc.dram_tensor("wqkv", [NQK * 128, NC * 128], F32, kind="ExternalInput").ap()
    wo = nc.dram_tensor("wo", [NC * 128, NC * 128], F32, kind="ExternalInput").ap()
    wffn = nc.dram_tensor("wffn", [NC * 128, NC * 128], F32, kind="ExternalInput").ap()
    cent = nc.dram_tensor("cent", [NC * 128, NCL], F32, kind="ExternalInput").ap()
    bqkv = nc.dram_tensor("bqkv", [128, NQK], F32, kind="ExternalInput").ap()
    bo_in = nc.dram_tensor("bo_in", [128, NC], F32, kind="ExternalInput").ap()
    bffn_in = nc.dram_tensor("bffn_in", [128, NC], F32, kind="ExternalInput").ap()
    asg_in = nc.dram_tensor("asg_in", [128, NC], I32, kind="ExternalInput").ap()
    bvq_in = nc.dram_tensor("bvq_in", [BH, HD_B], F32, kind="ExternalInput").ap()
    bvk_in = nc.dram_tensor("bvk_in", [BH, HD_B], F32, kind="ExternalInput").ap()
    bvv_in = nc.dram_tensor("bvv_in", [BH, HD_B], F32, kind="ExternalInput").ap()
    ident_in = nc.dram_tensor("ident_in", [128, 128], BF16, kind="ExternalInput").ap()
    tri_in = nc.dram_tensor("tri_in", [128, 128], BF16, kind="ExternalInput").ap()
    iota_in = nc.dram_tensor("iota_in", [128, NCL], F32, kind="ExternalInput").ap()
    onec_in = nc.dram_tensor("onec_in", [128, 1], BF16, kind="ExternalInput").ap()
    oner_in = nc.dram_tensor("oner_in", [65, 128], BF16, kind="ExternalInput").ap()
    onef_in = nc.dram_tensor("onef_in", [128, 128], BF16, kind="ExternalInput").ap()

    out = nc.dram_tensor("out", [D, S], F32, kind="ExternalOutput").ap()

    xs = nc.dram_tensor("xs", [D, S], F32, kind="Internal").ap()
    qkvs = nc.dram_tensor("qkvs", [NQK * 128, S], BF16, kind="Internal").ap()
    ctxs = nc.dram_tensor("ctxs", [NC * 128, S], BF16, kind="Internal").ap()
    ybs = nc.dram_tensor("ybs", [NC * 128, S], BF16, kind="Internal").ap()
    cnt_loc = nc.dram_tensor("cnt_loc", [128, NCL], F32, kind="Internal").ap()
    cnt_gl = [nc.dram_tensor(f"cnt_glob{i}", [128, NCL], F32, kind="Internal").ap()
              for i in range(2)]

    with ExitStack() as ctx:
        sb = lambda n, shp, dt: ctx.enter_context(nc.sbuf_tensor(n, shp, dt))
        psumt = lambda n, shp, dt: ctx.enter_context(nc.psum_tensor(n, shp, dt))

        YBW = max(NC * S,
                  10 * S + ((S // 128) + 4) * min(S, 1024),
                  (NC // BH) * S + (NC // BH) * 512 + (S // 128) * S + 2 * (S // 128) * 128)
        YB = sb("YB", [128, YBW], BF16)
        WT = sb("WT", [128, 2 * NC * 128], BF16)
        EV = sb("EV", [128, 2 * S], F32)
        CAST = sb("CAST", [128, 3 * S], BF16)
        XB = sb("XB", [128, 4 * S], BF16)
        BC = sb("BC", [128, 3 * S], BF16)
        identt = sb("identt", [128, 128], BF16)
        tri = sb("tri", [128, 128], BF16)
        iota_sb = sb("iota_sb", [128, NCL], F32)
        ones_c = sb("ones_c", [128, 1], BF16)
        ones_r = sb("ones_r", [65, 128], BF16)
        ones_f = sb("ones_f", [128, 128], BF16)
        bqkv_sb = sb("bqkv_sb", [128, NQK], F32)
        bo_sb = sb("bo_sb", [128, NC], F32)
        bffn_sb = sb("bffn_sb", [128, NC], F32)
        asg_i_sb = sb("asg_i_sb", [128, NC], I32)
        asg_f_sb = sb("asg_f_sb", [128, NC], F32)
        cq_sb = sb("cq_sb", [128, NC], F32)
        cv_sb = sb("cv_sb", [128, NC], F32)
        rg_sb = sb("rg_sb", [128, NC], F32)
        qgc_sb = sb("qgc_sb", [128, NT], BF16)
        cnt_sb = sb("cnt_sb", [128, NCL], F32)
        cgb_sb = sb("cgb_sb", [128, NCL], F32)
        ew_sb = sb("ew_sb", [128, 2 * NCL], F32)
        CENT = sb("CENT", [128, NC * NCL], BF16)
        hit_sb = BC[:, 2 * S:2 * S + NT * NCL]
        dlT_sb = CAST[:, 2 * S + NT * NCL:2 * S + 2 * NT * NCL]

        PSA = psumt("PSA", [128, 2048], F32)
        PSB = psumt("PSB", [128, 1024], F32)
        PSC = psumt("PSC", [128, 1024], F32)
        PSA16 = PSA[:, :].bitcast(BF16)
        PSB16 = PSB[:, :].bitcast(BF16)
        PSC16 = PSC[:, :].bitcast(BF16)

        p = Plan(nc, ctx)
        for s in ("hld0", "hld1", "hst0", "hst1", "gld0", "gld1",
                  "gst0", "gst1", "gaux", "lnl0", "lnl1", "lnl2", "lnl3", "spl", "pe", "dve", "act", "ccs"):
            p.sem(s)
        block = ctx.enter_context(nc.Block())

        BW = CAST[:, :].bitcast(F32)

        for dst, src in [(identt[:, :], ident_in), (tri[:, :], tri_in),
                         (iota_sb[:, :], iota_in), (ones_c[:, :], onec_in),
                         (ones_r[:, :], oner_in), (ones_f[:, :], onef_in), (bqkv_sb[:, :], bqkv),
                         (bo_sb[:, :], bo_in), (bffn_sb[:, :], bffn_in),
                         (asg_i_sb[:, :], asg_in),
                         (BW[0:BH, 0:HD_B], bvq_in),
                         (BW[0:BH, HD_B:2 * HD_B], bvk_in),
                         (BW[0:BH, 2 * HD_B:3 * HD_B], bvv_in)]:
            p.dma("sync", dst, src, "hld0")
        hld_consts = p.val("hld0")
        p.dma("gpsimd", CENT[:, :].rearrange("p (c n) -> p c n", n=NCL),
              cent.rearrange("(c p) n -> p c n", p=128), "gld0")
        gld_cent = p.val("gld0")
        p.dma("gpsimd", xs, x0, "gaux")
        gaux_xsinit = p.val("gaux")
        p.wait("sync", "hld0", hld_consts)
        p.wait("gpsimd", "gld0", gld_cent)

        p.wait("vector", "hld0", hld_consts)
        p.wait("scalar", "hld0", hld_consts)
        p.op("vector", lambda e: e.memset(WT[:, 0:2 * HD_B], 0.0), "dve")
        p.op("vector", lambda e: e.memset(cnt_sb[:, :], 0.0), "dve")
        alb = EV[0:BH, HD_B:HD_B + 4]
        for bi in range(3):
            w_ap = BW[0:BH, bi * HD_B:(bi + 1) * HD_B]
            p.red(alb[:, bi:bi + 1], w_ap, ALU.add, absval=True)
            p.wait("vector", "dve")
            p.ts(alb[:, bi:bi + 1], alb[:, bi:bi + 1], 1.0 / HD_B, None, ALU.mult)
            sgn = EV[0:BH, 0:HD_B]
            tmp = EV[0:BH, S:S + HD_B]
            p.wait("scalar", "dve", p.val("dve"))
            p.act(sgn, w_ap, AF.Sign)
            p.wait("vector", "act")
            p.tt(tmp, sgn, sgn, ALU.mult)
            p.wait("vector", "dve")
            p.ts(tmp, tmp, -1.0, 1.0, ALU.mult, ALU.add)
            p.wait("vector", "dve")
            p.tt(tmp, sgn, tmp, ALU.add)
            p.wait("vector", "dve")
            p.ts(w_ap, tmp, alb[:, bi:bi + 1], None, ALU.mult)
            p.wait("vector", "dve")
        p.stt(WT[0:BH, 0:HD_B], BW[0:BH, 0:HD_B], B_SCALE,
              BW[0:BH, HD_B:2 * HD_B], ALU.mult, ALU.mult)
        p.copy(WT[0:BH, HD_B:2 * HD_B], BW[0:BH, 2 * HD_B:3 * HD_B])
        p.wait("vector", "dve")
        dve_binds = p.val("dve")
        p.wait("tensor", "dve", dve_binds)
        p.wait("tensor", "hld0", hld_consts)
        for j in range(BDC):
            p.tp(PSB16[:, j * 128:(j + 1) * 128], WT[:, j * 128:(j + 1) * 128], identt[:, :])
        for j in range(BDC):
            p.tp(PSC16[:, j * 128:(j + 1) * 128], WT[:, HD_B + j * 128:HD_B + (j + 1) * 128],
                 identt[:, :], inc="pe" if j == BDC - 1 else None)
        pe_binds = p.val("pe")
        p.wait("vector", "pe", pe_binds)
        for j in range(BDC):
            p.copy(cq_sb[:, j * BH:(j + 1) * BH], PSB16[:, j * 128:j * 128 + BH], inc=None)
            p.copy(cv_sb[:, j * BH:(j + 1) * BH], PSC16[:, j * 128:j * 128 + BH], inc=None)
        p.copy(asg_f_sb[:, :], asg_i_sb[:, :])
        p.wait("vector", "dve")
        prolog_waits = [("dve", p.val("dve")), ("act", p.val("act"))]
        p.barrier()

        wstate = {"k": 0}
        wt_war = {-2: pe_binds, -1: pe_binds}

        def weight_tile_load(src_rows):
            k = wstate["k"]
            slot = k % 2
            dst = WT[:, slot * (NC * 128):(slot + 1) * (NC * 128)]
            p.wait("gpsimd", "pe", wt_war[k - 2])
            v = p.dma("gpsimd", dst, src_rows, "gld0" if slot == 0 else "gld1")
            wstate["k"] += 1
            return dst, v, k

        def qkv_psum_views(buf):
            views = []
            for ns in range(NS):
                if buf == 0:
                    views.append(PSA[:, ns * 512:(ns + 1) * 512])
                else:
                    t = PSB if ns < 2 else PSC
                    views.append(t[:, (ns % 2) * 512:((ns % 2) + 1) * 512])
            return views

        def qkv_psum_spans(buf):
            if buf == 0:
                return [(PSA[:, 0:S], 0, S)]
            spans = [(PSB[:, 0:min(S, 1024)], 0, min(S, 1024))]
            if S > 1024:
                spans.append((PSC[:, 0:S - 1024], 1024, S - 1024))
            return spans

        def layernorm(src_dram, extra_waits=(), chunk_waits=None, hooks=None):
            hld_x, act_sq, pe_done = [], [], []
            for dc in range(NC):
                slot = dc % 4
                lsem = f"lnl{slot}"
                xb = XB[:, slot * S:(slot + 1) * S]
                if dc < 4 and hooks is not None:
                    pos = hooks[dc] + 2 * dc
                    hld_x.append(p.dma_at("gpsimd", xb,
                                          src_dram[dc * 128:(dc + 1) * 128, :],
                                          lsem, at=pos))
                    if chunk_waits is not None:
                        for sname, v in chunk_waits(dc):
                            h2, _ = p.sems[sname]
                            p.ops["gpsimd"].insert(
                                pos, lambda e, h2=h2, v=v: e.wait_ge(h2, v))
                else:
                    if dc >= 4:
                        p.wait("gpsimd", "pe", pe_done[dc - 4])
                        p.wait("gpsimd", "dve", act_sq[dc - 4])
                    if dc == 0:
                        for sname, v in extra_waits:
                            p.wait("gpsimd", sname, v)
                    if chunk_waits is not None:
                        for sname, v in chunk_waits(dc):
                            p.wait("gpsimd", sname, v)
                    hld_x.append(p.dma("gpsimd", xb,
                                       src_dram[dc * 128:(dc + 1) * 128, :], lsem))
                sq = CAST[:, (dc % 2) * S:(dc % 2 + 1) * S]
                p.wait("vector", lsem, hld_x[dc])
                if dc >= 2:
                    p.wait("vector", "pe", pe_done[dc - 2])
                act_sq.append(p.tt(sq, xb, xb, ALU.mult))
                p.wait("tensor", lsem, hld_x[dc])
                p.wait("tensor", "dve", act_sq[dc])
                st, sp = (dc == 0), (dc == NC - 1)
                for ns in range(NS):
                    p.mm(PSA[:, ns * 512:(ns + 1) * 512], ones_f[:, :],
                         xb[:, ns * 512:(ns + 1) * 512], st, sp)
                for ns in range(NS):
                    t = PSB if ns < 2 else PSC
                    p.mm(t[:, (ns % 2) * 512:((ns % 2) + 1) * 512], ones_f[:, :],
                         sq[:, ns * 512:(ns + 1) * 512], st, sp,
                         inc="pe" if ns == NS - 1 else None)
                pe_done.append(p.val("pe"))
            mbc = EV[:, 0:S]
            e2 = EV[:, S:2 * S]
            p.wait("vector", "pe", pe_done[-1])
            p.ts(mbc, PSA[:, 0:S], 1.0 / D, None, ALU.mult)
            if S > 1024:
                p.ts(e2[:, 0:1024], PSB[:, 0:1024], 1.0 / D, None, ALU.mult, inc=None)
                p.ts(EV[:, S + 1024:2 * S], PSC[:, 0:S - 1024], 1.0 / D, None, ALU.mult)
            else:
                p.ts(e2, PSB[:, 0:S], 1.0 / D, None, ALU.mult)
            p.wait("vector", "dve")
            p.tt(PSA[:, 0:S], mbc, mbc, ALU.mult)
            p.wait("vector", "dve")
            p.tt(e2, e2, PSA[:, 0:S], ALU.subtract)
            p.wait("vector", "dve")
            p.ts(e2, e2, EPS, None, ALU.add)
            p.wait("vector", "dve")
            dve_var = p.val("dve")
            p.wait("scalar", "dve", dve_var)
            p.act(PSA[:, 0:S], e2, AF.Ln)
            p.wait("scalar", "act")
            p.act(BC[:, 0:S], PSA[:, 0:S], AF.Exp, scale=-0.5)
            p.wait("vector", "act")
            p.copy(BC[:, S:2 * S], mbc)
            p.wait("vector", "dve")
            return pe_done, act_sq

        def ln_apply(src_dram, ydest_fn, spill=None, stats=None):
            hld_x, dve_y, st_y = [], [], []
            act_entry = p.val("act")
            for dc in range(NC):
                slot = dc % 4
                lsem = f"lnl{slot}"
                if dc < 4 and stats is not None:
                    pe_done, sq_done = stats
                    p.wait("gpsimd", "pe", pe_done[NC - 4 + dc])
                    p.wait("gpsimd", "dve", sq_done[NC - 4 + dc])
                if dc >= 4:
                    p.wait("gpsimd", "dve", dve_y[dc - 4])
                xb = XB[:, slot * S:(slot + 1) * S]
                hld_x.append(p.dma("gpsimd", xb,
                                   src_dram[dc * 128:(dc + 1) * 128, :], lsem))
                t16 = CAST[:, (dc % 2) * S:(dc % 2 + 1) * S]
                dst = ydest_fn(dc)
                p.wait("vector", lsem, hld_x[dc])
                if dc == 0:
                    p.wait("vector", "act", act_entry)
                p.tt(t16, xb, BC[:, S:2 * S], ALU.subtract)
                p.wait("vector", "dve")
                dve_y.append(p.tt(dst, t16, BC[:, 0:S], ALU.mult))
                if spill is not None:
                    p.wait("sync", "dve", dve_y[dc])
                    st_y.append(spill(dc, dst))
            return dve_y, st_y

        def dense_phase(w_src, n_fc, rhs_fn, evac_fn, first_waits=()):
            tiles = [weight_tile_load(w_src[0:128, :])]
            if n_fc > 1:
                tiles.append(weight_tile_load(w_src[128:256, :]))
            evac_dve = []
            for fc in range(n_fc):
                dst, gv, k = tiles[fc]
                buf = k % 2
                views = qkv_psum_views(buf)
                p.wait("tensor", "gld0" if buf == 0 else "gld1", gv)
                if fc < 2:
                    for sname, v in first_waits:
                        p.wait("tensor", sname, v)
                if fc >= 2:
                    p.wait("tensor", "dve", evac_dve[fc - 2])
                for dc in range(NC):
                    lhsT = dst[:, dc * 128:(dc + 1) * 128]
                    st, sp = (dc == 0), (dc == NC - 1)
                    for ns in range(NS):
                        p.mm(views[ns], lhsT, rhs_fn(dc)[:, ns * 512:(ns + 1) * 512],
                             st, sp, inc="pe" if (sp and ns == NS - 1) else None)
                wt_war[k] = p.val("pe")
                if fc + 2 < n_fc:
                    tiles.append(weight_tile_load(w_src[(fc + 2) * 128:(fc + 3) * 128, :]))
                evac_dve.append(evac_fn(fc, qkv_psum_spans(buf), p.val("pe")))

        lnstats1 = layernorm(x0, extra_waits=prolog_waits)
        ln_apply(x0, lambda dc: YB[:, dc * S:(dc + 1) * S], stats=lnstats1)
        p.barrier()

        a3_hst = {0: [], 1: []}

        def a3_evac(fc, spans, pe_cnt):
            slot = fc % 2
            hsem = "hst0" if slot == 0 else "hst1"
            tdst = CAST[:, slot * S:(slot + 1) * S]
            p.wait("vector", "pe", pe_cnt)
            if a3_hst[slot]:
                p.wait("vector", hsem, a3_hst[slot][-1])
            for (pv, c0, w) in spans:
                p.ts(tdst[:, c0:c0 + w], pv, bqkv_sb[:, fc:fc + 1], None, ALU.add,
                     inc="dve" if c0 + w == S else None)
            dve_ev = p.val("dve")
            p.wait("sync", "dve", dve_ev)
            a3_hst[slot].append(
                p.dma("sync", qkvs[fc * 128:(fc + 1) * 128, :], tdst, hsem))
            return dve_ev

        dense_phase(wqkv, NQK, lambda dc: YB[:, dc * S:(dc + 1) * S], a3_evac)
        a3_stores = [("hst0", p.val("hst0")), ("hst1", p.val("hst1"))]
        p.barrier()

        EOFF = 10 * S
        a4_hld = []
        a4_head_pe = []
        a4_ctx_st = []

        def a4_load(h):
            slot = h % 2
            hsem = "hld0" if slot == 0 else "hld1"
            st = slot * 5 * S
            if h >= 2:
                p.wait("sync", "pe", a4_head_pe[h - 2])
            if h == 0:
                for sname, v in a3_stores:
                    p.wait("sync", sname, v)
            v = None
            for i, base in enumerate((0, NC, 2 * NC)):
                v = p.dma("sync", YB[:, st + i * S:st + (i + 1) * S],
                          qkvs[(base + h) * 128:(base + h + 1) * 128, :], hsem)
            return v

        a4_hld.append(a4_load(0))
        a4_state = {"prev_ctx": None, "prev_rr": None, "pend": None, "pe_sums": None,
                    "gidx": 0, "prev_acc": None}
        act_hist = []
        EOFF2 = EOFF + NT * QH

        def a4_tail():
            pend = a4_state["pend"]
            if pend is None:
                return
            h_, hq_, cx_, q0_, pe_ctx, dve_acc = pend
            a4_state["prev_acc"] = dve_acc
            p.wait("tensor", "dve", dve_acc)
            for j in range(4):
                ea = YB[:, EOFF2 + j * QH:EOFF2 + (j + 1) * QH]
                for ns in range(NQS):
                    p.mm(PSA[:, ns * 512:(ns + 1) * 512], ones_f[:, :],
                         ea[:, ns * 512:(ns + 1) * 512], j == 0, j == 3,
                         inc="pe" if (j == 3 and ns == NQS - 1) else None)
            a4_state["pe_sums"] = p.val("pe")
            p.wait("scalar", "pe", a4_state["pe_sums"])
            if a4_state["prev_ctx"] is not None:
                p.wait("scalar", "dve", a4_state["prev_ctx"])
            p.act(EV[:, 0:QH], PSA[:, 0:QH], AF.Ln)
            a4_state["prev_rr"] = p.val("act")
            p.wait("scalar", "act")
            p.act(CAST[:, 0:QH], EV[:, 0:QH], AF.Exp, scale=-1.0)
            act_rr = p.val("act")
            p.wait("vector", "act", act_rr)
            if h_ >= 2 and hq_ == 0:
                p.wait("vector", "hst0" if h_ % 2 == 0 else "hst1", a4_ctx_st[h_ - 2])
            p.tt(cx_[:, q0_:q0_ + QH], PSA[:, 1024:1024 + QH], CAST[:, 0:QH], ALU.mult)
            p.wait("vector", "dve")
            a4_state["prev_ctx"] = p.val("dve")
            if hq_ == NHALF - 1:
                a4_head_pe.append(pe_ctx)
                p.wait("sync", "dve", a4_state["prev_ctx"])
                a4_ctx_st.append(p.dma("sync", ctxs[h_ * 128:(h_ + 1) * 128, :], cx_,
                                       "hst0" if h_ % 2 == 0 else "hst1"))
            a4_state["pend"] = None

        def a4_sbuf(g):
            r = g % 3
            if r == 0:
                return PSB[:, 0:QH]
            if r == 1:
                return PSC[:, 0:QH]
            return PSA[:, 0:QH]

        for h in range(HEADS):
            slot = h % 2
            hsem = "hld0" if slot == 0 else "hld1"
            st = slot * 5 * S
            qv = YB[:, st:st + S]
            kv = YB[:, st + S:st + 2 * S]
            vf = YB[:, st + 2 * S:st + 3 * S]
            vt = YB[:, st + 3 * S:st + 4 * S]
            cx = YB[:, st + 4 * S:st + 5 * S]
            p.wait("tensor", hsem, a4_hld[h])
            if act_hist:
                p.wait("tensor", "act", act_hist[-1])
            for tt in range(NT):
                p.tp(PSB16[:, tt * 128:(tt + 1) * 128],
                     vf[:, tt * 128:(tt + 1) * 128], identt[:, :],
                     inc="pe" if tt == NT - 1 else None)
            p.wait("vector", "pe", p.val("pe"))
            p.copy(vt[:, 0:NT * 128], PSB16[:, 0:NT * 128])
            p.wait("vector", "dve")
            dve_vt = p.val("dve")

            for hq in range(NHALF):
                q0 = hq * QH
                act_cur = []

                def emit_scores(sc, q0=q0, qv=qv, kv=kv, act_cur=act_cur,
                                dve_vt=dve_vt, hq=hq):
                    g = a4_state["gidx"]
                    a4_state["gidx"] = g + 1
                    sps = a4_sbuf(g)
                    if len(act_hist) >= 3:
                        p.wait("tensor", "act", act_hist[-3])
                    if sc == 0 and hq == 0:
                        p.wait("tensor", "dve", dve_vt)
                    if g % 3 == 2 and a4_state["prev_rr"] is not None:
                        p.wait("tensor", "act", a4_state["prev_rr"])
                    for ns in range(NQS):
                        p.mm(sps[:, ns * 512:(ns + 1) * 512],
                             kv[:, sc * 128:(sc + 1) * 128],
                             qv[:, q0 + ns * 512:q0 + (ns + 1) * 512],
                             True, True, inc="pe" if ns == NQS - 1 else None)
                    p.wait("scalar", "pe", p.val("pe"))
                    if sc == 0 and a4_state["prev_acc"] is not None:
                        p.wait("scalar", "dve", a4_state["prev_acc"])
                    eb = YB[:, EOFF + sc * QH:EOFF + (sc + 1) * QH]
                    v = p.act(eb, sps[:, 0:QH], AF.Exp, scale=SCALE_A)
                    act_cur.append(v)
                    act_hist.append(v)

                a4_tail()
                emit_scores(0)
                if NT > 1:
                    emit_scores(1)
                if NT > 2:
                    emit_scores(2)
                if hq == 0 and h + 1 < HEADS:
                    a4_hld.append(a4_load(h + 1))
                for sc in range(NT):
                    j = sc % 4
                    ea = YB[:, EOFF2 + j * QH:EOFF2 + (j + 1) * QH]
                    eb = YB[:, EOFF + sc * QH:EOFF + (sc + 1) * QH]
                    p.wait("vector", "act", act_cur[sc])
                    if sc < 4:
                        if a4_state["pe_sums"] is not None:
                            p.wait("vector", "pe", a4_state["pe_sums"])
                        p.copy(ea, eb)
                    else:
                        p.tt(ea, ea, eb, ALU.add)
                    p.wait("vector", "dve")
                    p.wait("tensor", "act", act_cur[sc])
                    if sc == 0 and a4_state["prev_ctx"] is not None:
                        p.wait("tensor", "dve", a4_state["prev_ctx"])
                    st_, sp_ = (sc == 0), (sc == NT - 1)
                    for ns in range(NQS):
                        p.mm(PSA[:, 1024 + ns * 512:1024 + (ns + 1) * 512],
                             vt[:, sc * 128:(sc + 1) * 128],
                             eb[:, ns * 512:(ns + 1) * 512], st_, sp_,
                             inc="pe" if ns == NQS - 1 else None)
                    if sc + 3 < NT:
                        emit_scores(sc + 3)
                a4_state["pend"] = (h, hq, cx, q0, p.val("pe"), p.val("dve"))
        a4_tail()
        a4_stores = [("hst0", p.val("hst0")), ("hst1", p.val("hst1"))]
        p.barrier()
        a5_acc_rec = []
        a5_hooks = []

        for sname, v in a4_stores:
            p.wait("sync", sname, v)
        p.wait("sync", "pe", a4_head_pe[-1])
        p.wait("sync", "dve", p.val("dve"))
        ctx_ld = None
        for dc in range(NC):
            ctx_ld = p.dma("sync", YB[:, dc * S:(dc + 1) * S],
                           ctxs[dc * 128:(dc + 1) * 128, :], "hld0")
        p.wait("tensor", "hld0", ctx_ld)
        a5_gst = {0: [], 1: []}

        def a5_evac(fc, spans, pe_cnt):
            slot = fc % 2
            gsem = "gst0" if slot == 0 else "gst1"
            tdst = EV[:, slot * S:(slot + 1) * S]
            p.wait("vector", "pe", pe_cnt)
            if a5_gst[slot]:
                p.wait("vector", gsem, a5_gst[slot][-1])
            for (pv, c0, w) in spans:
                p.ts(tdst[:, c0:c0 + w], pv, bo_sb[:, fc:fc + 1], None, ALU.add,
                     inc="dve" if c0 + w == S else None)
            dve_ev = p.val("dve")
            p.wait("gpsimd", "dve", dve_ev)
            if fc == 0:
                p.wait("gpsimd", "gaux", gaux_xsinit)
            a5_gst[slot].append(
                p.dma("gpsimd", xs[fc * 128:(fc + 1) * 128, :], tdst, gsem, accum=True))
            a5_acc_rec.append((gsem, a5_gst[slot][-1]))
            if fc < 4:
                a5_hooks.append(p.mark("gpsimd"))
            return dve_ev

        dense_phase(wo, NC, lambda dc: YB[:, dc * S:(dc + 1) * S], a5_evac)
        a5_accums = [("gst0", p.val("gst0")), ("gst1", p.val("gst1"))]
        p.barrier()

        def ffn_phase(call_idx, accum_target, acc_rec, copy_xs_to_out, hooks=None):
            lnstats = layernorm(xs, chunk_waits=lambda dc: [acc_rec[dc]], hooks=hooks)
            dve_y, _ = ln_apply(xs, lambda dc: YB[:, dc * S:(dc + 1) * S],
                                stats=lnstats)
            p.wait("tensor", "gld0", gld_cent)
            for dc in range(NC):
                st, sp = (dc == 0), (dc == NC - 1)
                p.wait("tensor", "dve", dve_y[dc])
                for ns in range(NS):
                    p.mm(PSA[0:NCL, ns * 512:(ns + 1) * 512],
                         CENT[:, dc * NCL:(dc + 1) * NCL],
                         YB[:, dc * S + ns * 512:dc * S + (ns + 1) * 512],
                         st, sp, inc="pe" if (sp and ns == NS - 1) else None)
            pe_dl = p.val("pe")
            dl16 = CAST[0:NCL, 0:S]
            p.wait("vector", "pe", pe_dl)
            p.copy(dl16, PSA[0:NCL, 0:S])
            p.wait("vector", "dve")
            dve_dl = p.val("dve")
            p.wait("tensor", "dve", dve_dl)
            for tt in range(NT):
                p.tp(PSB16[:, tt * NCL:(tt + 1) * NCL], dl16[:, tt * 128:(tt + 1) * 128],
                     identt[0:NCL, 0:NCL], inc="pe" if tt == NT - 1 else None)
            p.wait("vector", "pe", p.val("pe"))
            p.copy(dlT_sb[:, 0:NT * NCL], PSB16[:, 0:NT * NCL])
            p.wait("vector", "dve")
            dlt_all = dlT_sb[:, 0:NT * NCL]
            d3 = lambda ap: ap.rearrange("p (g c) -> p g c", c=NCL)
            col3 = lambda ap: ap.rearrange("p (g o) -> p g o", o=1)
            mx = EV[:, 3 * 512:3 * 512 + NT]
            Ssum = EV[:, 3 * 512 + NT:3 * 512 + 2 * NT]
            thr = EV[:, 3 * 512 + 2 * NT:3 * 512 + 3 * NT]
            qm = EV[:, 3 * 512 + 3 * NT:3 * 512 + 4 * NT]
            dtile = CAST[:, 2 * S:2 * S + NT * NCL]
            etile = EV[:, 0:NT * NCL]
            util = EV[:, S:S + NT * NCL]
            p.red(col3(mx), d3(dlt_all), ALU.max)
            p.wait("vector", "dve")
            p.tt(d3(dtile), d3(dlt_all), col3(mx).broadcast_to((128, NT, NCL)),
                 ALU.subtract)
            p.wait("vector", "dve")
            p.wait("scalar", "dve", p.val("dve"))
            p.act(etile, dtile, AF.Exp, scale=1.0 / TEMP)
            p.wait("vector", "act")
            p.red(col3(Ssum), d3(etile), ALU.add)
            p.wait("vector", "dve")
            p.ts(thr, Ssum, THR, None, ALU.mult)
            p.wait("vector", "dve")
            p.tt(d3(util), d3(etile), col3(thr).broadcast_to((128, NT, NCL)),
                 ALU.subtract)
            p.wait("vector", "dve")
            p.tss(hit_sb, util, 0.0, ALU.is_gt)
            p.red(col3(qm), d3(util), ALU.max)
            p.wait("vector", "dve")
            p.tss(qgc_sb[:, 0:NT], qm, 0.0, ALU.is_gt)
            p.wait("vector", "dve")
            dve_hits = p.val("dve")
            p.wait("tensor", "dve", dve_hits)
            for ns in range(NT * NCL // 512):
                p.mm(PSC[:, ns * 512:(ns + 1) * 512], ones_f[:, :],
                     hit_sb[:, ns * 512:(ns + 1) * 512], True, True,
                     inc="pe" if ns == NT * NCL // 512 - 1 else None)
            p.wait("vector", "pe", p.val("pe"))
            p.red(cnt_sb[:, :].rearrange("p (c o) -> p c o", o=1),
                  PSC[:, 0:NT * NCL].rearrange("p (g c) -> p g c", c=NCL).transpose([0, 2, 1]),
                  ALU.add)
            p.wait("vector", "dve")
            dve_cnt = p.val("dve")
            p.wait("gpsimd", "gaux", p.val("gaux"))
            p.wait("gpsimd", "dve", dve_cnt)
            p.dma("gpsimd", cnt_loc, cnt_sb[:, :], "gaux")
            p.wait("gpsimd", "gaux", p.val("gaux"))
            p.op("gpsimd", lambda e, o=cnt_gl[call_idx]: e.collective_compute(
                "AllReduce", ALU.add, replica_groups=[list(range(N_CORES))],
                ins=[cnt_loc.opt()], outs=[o.opt()]), "ccs", 1)
            p.wait("gpsimd", "ccs", p.val("ccs"))
            p.dma("gpsimd", cgb_sb[:, :], cnt_gl[call_idx], "gaux")
            gaux_cntg = p.val("gaux")
            p.wait("tensor", "dve", dve_cnt)
            for tt in range(NT):
                p.tp(PSB16[0:1, tt * 128:(tt + 1) * 128], qgc_sb[:, tt:tt + 1],
                     identt[:, :], inc="pe" if tt == NT - 1 else None)
            qg16 = CAST[0:1, 2 * S:3 * S]
            p.wait("vector", "pe", p.val("pe"))
            for tt in range(NT):
                p.copy(qg16[:, tt * 128:(tt + 1) * 128],
                       PSB16[0:1, tt * 128:(tt + 1) * 128],
                       inc="dve" if tt == NT - 1 else None)
            p.wait("vector", "dve")
            dve_qgrow = p.val("dve")
            p.wait("tensor", "dve", dve_qgrow)
            for ns in range(NS):
                p.mm(PSA[:, ns * 512:(ns + 1) * 512], ones_r[0:1, :],
                     qg16[:, ns * 512:(ns + 1) * 512], True, True,
                     inc="pe" if ns == NS - 1 else None)
            p.wait("vector", "pe", p.val("pe"))
            p.copy(BC[:, 2 * S:3 * S], PSA[:, 0:S])
            p.wait("vector", "dve")
            dve_qgbc = p.val("dve")
            p.wait("vector", "gaux", gaux_cntg)
            dve_psfree = dve_qgbc
            for fc in range(NC):
                p.ts(ew_sb[:, 0:NCL], iota_sb[:, :], asg_f_sb[:, fc:fc + 1], None,
                     ALU.subtract)
                p.wait("vector", "dve")
                p.tss(ew_sb[:, NCL:2 * NCL], ew_sb[:, 0:NCL], 0.0, ALU.is_equal)
                p.wait("vector", "dve")
                p.tt(ew_sb[:, 0:NCL], ew_sb[:, NCL:2 * NCL], cgb_sb[:, :], ALU.mult)
                p.wait("vector", "dve")
                p.red(EV[:, 0:1], ew_sb[:, 0:NCL], ALU.add)
                p.wait("vector", "dve")
                p.tss(rg_sb[:, fc:fc + 1], EV[:, 0:1], 0.0, ALU.is_gt)
                p.wait("vector", "dve")
            dve_gates = p.val("dve")
            copy_done = None
            if copy_xs_to_out:
                for sname, v in (acc_rec[-1], acc_rec[-2]):
                    p.wait("gpsimd", sname, v)
                p.wait("gpsimd", "gaux", p.val("gaux"))
                p.dma("gpsimd", out, xs, "gaux")
                copy_done = p.val("gaux")
            f_gst = {0: [], 1: []}
            f_hooks = []

            def f_evac(fc, spans, pe_cnt2):
                slot = fc % 2
                gsem = "gst0" if slot == 0 else "gst1"
                t2 = EV[:, slot * S:(slot + 1) * S]
                p.wait("vector", "pe", pe_cnt2)
                p.wait("vector", "dve", dve_gates)
                if f_gst[slot]:
                    p.wait("vector", gsem, f_gst[slot][-1])
                for (pv, c0, w) in spans:
                    p.ts(t2[:, c0:c0 + w], pv, bffn_sb[:, fc:fc + 1],
                         rg_sb[:, fc:fc + 1], ALU.add, ALU.mult,
                         inc="dve" if c0 + w == S else None)
                dve_ev = p.val("dve")
                p.wait("vector", "dve")
                p.tt(t2, t2, BC[:, 2 * S:3 * S], ALU.mult)
                p.wait("gpsimd", "dve", p.val("dve"))
                if fc == 0 and copy_done is not None:
                    p.wait("gpsimd", "gaux", copy_done)
                f_gst[slot].append(
                    p.dma("gpsimd", accum_target[fc * 128:(fc + 1) * 128, :],
                          t2, gsem, accum=True))
                f_acc_rec.append((gsem, f_gst[slot][-1]))
                if fc < 4:
                    f_hooks.append(p.mark("gpsimd"))
                return dve_ev

            f_acc_rec = []
            dense_phase(wffn, NC, lambda dc: YB[:, dc * S:(dc + 1) * S], f_evac,
                        first_waits=[("dve", dve_psfree)])
            f_hooks_out[0] = f_hooks
            return f_acc_rec

        f_hooks_out = [None]
        ffn1_acc_rec = ffn_phase(0, xs, a5_acc_rec, copy_xs_to_out=False, hooks=a5_hooks)
        p.barrier()

        lnstats_b = layernorm(xs, chunk_waits=lambda dc: [ffn1_acc_rec[dc]],
                              hooks=f_hooks_out[0])
        _, yb_st = ln_apply(xs, lambda dc: YB[:, dc * S:(dc + 1) * S],
                            spill=lambda dc, dst: p.dma(
                                "sync", ybs[dc * 128:(dc + 1) * 128, :], dst, "spl"),
                            stats=lnstats_b)
        p.barrier()

        QP0 = BDC * S
        A0 = QP0 + BDC * 512
        Y0 = A0 + NT * S
        NQ = S // 512
        b_gst = {0: [], 1: []}
        b_head_pe = None
        b_head_dve = None
        b_acc_rec = []
        b_hooks = []
        for h in range(BH):
            p.wait("sync", "spl", yb_st[-1])
            if b_head_pe is not None:
                p.wait("sync", "pe", b_head_pe)
                p.wait("sync", "dve", b_head_dve)
            yh_ld = p.dma("sync", YB[:, 0:BDC * S].rearrange("p (c n) -> p c n", n=S),
                          ybs[h * BDC * 128:(h + 1) * BDC * 128, :].rearrange(
                              "(c p) n -> p c n", p=128), "hld0")
            p.wait("vector", "hld0", yh_ld)
            p.wait("tensor", "hld0", yh_ld)
            sc_idx = 0
            act_sig = []
            pe_prev_tq = None
            for tq in range(NQ):
                t0 = tq * 512
                if pe_prev_tq is not None:
                    p.wait("vector", "pe", pe_prev_tq)
                for dc in range(BDC):
                    p.ts(YB[:, QP0 + dc * 512:QP0 + (dc + 1) * 512],
                         YB[:, dc * S + t0:dc * S + t0 + 512],
                         cq_sb[:, dc * BH + h:dc * BH + h + 1], None, ALU.mult,
                         inc="dve" if dc == BDC - 1 else None)
                dve_qp = p.val("dve")
                p.wait("tensor", "dve", dve_qp)
                smax = (t0 + 512) // 128
                for sc in range(smax):
                    c0 = max(t0, sc * 128)
                    w = t0 + 512 - c0
                    sps = PSB[:, 0:512] if sc_idx % 2 == 0 else PSB[:, 512:1024]
                    if sc_idx >= 2:
                        p.wait("tensor", "act", act_sig[sc_idx - 2])
                    for dc in range(BDC):
                        p.mm(sps[:, 0:w],
                             YB[:, dc * S + sc * 128:dc * S + (sc + 1) * 128],
                             YB[:, QP0 + dc * 512 + (c0 - t0):QP0 + (dc + 1) * 512],
                             dc == 0, dc == BDC - 1,
                             inc="pe" if dc == BDC - 1 else None)
                    p.wait("scalar", "pe", p.val("pe"))
                    adst = YB[:, A0 + sc * S + c0:A0 + sc * S + t0 + 512]
                    act_sig.append(p.act(adst, sps[:, 0:w], AF.Sigmoid))
                    if c0 == sc * 128:
                        p.wait("vector", "act", act_sig[-1])
                        p.tt(YB[:, A0 + sc * S + c0:A0 + sc * S + c0 + 128],
                             YB[:, A0 + sc * S + c0:A0 + sc * S + c0 + 128],
                             tri[:, :], ALU.mult)
                    sc_idx += 1
                pe_prev_tq = p.val("pe")
            dve_attn = p.val("dve")
            act_attn = p.val("act")
            dve_ytcp = []

            def b_transposes(dc):
                psT = PSC16 if dc % 2 == 0 else PSB16
                p.wait("tensor", "dve", dve_attn)
                p.wait("tensor", "act", act_attn)
                if dc >= 2:
                    p.wait("tensor", "dve", dve_ytcp[dc - 2])
                for sc in range(NT):
                    p.tp(psT[:, sc * 128:(sc + 1) * 128],
                         YB[:, dc * S + sc * 128:dc * S + (sc + 1) * 128],
                         identt[:, :], inc="pe" if sc == NT - 1 else None)
                p.wait("vector", "pe", p.val("pe"))
                p.copy(YB[:, Y0 + (dc % 2) * NT * 128:Y0 + (dc % 2 + 1) * NT * 128],
                       psT[:, 0:NT * 128])
                p.wait("vector", "dve")
                dve_ytcp.append(p.val("dve"))

            b_transposes(0)
            b_evac_dve = []
            for dc in range(BDC):
                if dc + 1 < BDC:
                    b_transposes(dc + 1)
                yt0 = Y0 + (dc % 2) * NT * 128
                p.wait("tensor", "dve", dve_ytcp[dc])
                if dc >= 1:
                    p.wait("tensor", "dve", b_evac_dve[dc - 1])
                for sc in range(NT):
                    c = sc * 128
                    while c < S:
                        w = min(512 - (c % 512), S - c)
                        j = c // 512
                        sp = (sc == min(NT - 1, 4 * j + 3))
                        p.mm(PSA[:, c:c + w], YB[:, yt0 + sc * 128:yt0 + (sc + 1) * 128],
                             YB[:, A0 + sc * S + c:A0 + sc * S + c + w],
                             sc == 0, sp,
                             inc="pe" if (sc == NT - 1 and c + w == S) else None)
                        c += w
                pe_ctx = p.val("pe")
                gdc = h * BDC + dc
                slot = gdc % 2
                gsem = "gst0" if slot == 0 else "gst1"
                t2 = EV[:, slot * S:(slot + 1) * S]
                p.wait("vector", "pe", pe_ctx)
                if b_gst[slot]:
                    p.wait("vector", gsem, b_gst[slot][-1])
                p.ts(t2, PSA[:, 0:S], cv_sb[:, dc * BH + h:dc * BH + h + 1], None,
                     ALU.mult)
                b_evac_dve.append(p.val("dve"))
                p.wait("gpsimd", "dve", p.val("dve"))
                b_gst[slot].append(
                    p.dma("gpsimd", xs[gdc * 128:(gdc + 1) * 128, :], t2, gsem,
                          accum=True))
                b_acc_rec.append((gsem, b_gst[slot][-1]))
                if gdc < 4:
                    b_hooks.append(p.mark("gpsimd"))
            b_head_pe = p.val("pe")
            b_head_dve = p.val("dve")
        p.barrier()

        ffn_phase(1, out, b_acc_rec, copy_xs_to_out=True, hooks=b_hooks)

        for sname in ("gst0", "gst1", "gaux", "hst0", "hst1"):
            p.wait("sync", sname, p.val(sname))
            p.wait("gpsimd", sname, p.val(sname))

        p.emit(block)

    return nc


def _prep_inputs(L0_attn_a_x, L0_attn_a_attn_qkvw, L0_attn_a_attn_qkvb,
                 L0_attn_a_attn_ow, L0_attn_a_attn_ob, L0_attn_b_bv_q,
                 L0_attn_b_bv_k, L0_attn_b_bv_v, L0_ffn_weight, L0_ffn_bias,
                 L0_ffn_centroids, L0_ffn_assignments, D, S):
    NC = D // 128
    NQK = 3 * NC
    f32 = np.float32
    wqkv_a = np.ascontiguousarray(L0_attn_a_attn_qkvw, dtype=f32)
    wo_a = np.ascontiguousarray(L0_attn_a_attn_ow, dtype=f32)
    wffn_t = np.ascontiguousarray(np.asarray(L0_ffn_weight, dtype=f32).T)

    def blk(w, nfc):
        return np.ascontiguousarray(
            w.reshape(NC, 128, nfc, 128).transpose(2, 1, 0, 3).reshape(nfc * 128, NC * 128))

    shared = {
        "wqkv": blk(wqkv_a, NQK),
        "wo": blk(wo_a, NC),
        "wffn": blk(wffn_t, NC),
        "cent": np.ascontiguousarray(np.asarray(L0_ffn_centroids, dtype=f32).T),
        "bqkv": np.ascontiguousarray(np.asarray(L0_attn_a_attn_qkvb, f32).reshape(NQK, 128).T),
        "bo_in": np.ascontiguousarray(np.asarray(L0_attn_a_attn_ob, f32).reshape(NC, 128).T),
        "bffn_in": np.ascontiguousarray(np.asarray(L0_ffn_bias, f32).reshape(NC, 128).T),
        "asg_in": np.ascontiguousarray(
            np.asarray(L0_ffn_assignments, np.int32).reshape(NC, 128).T),
        "bvq_in": np.ascontiguousarray(L0_attn_b_bv_q, dtype=f32),
        "bvk_in": np.ascontiguousarray(L0_attn_b_bv_k, dtype=f32),
        "bvv_in": np.ascontiguousarray(L0_attn_b_bv_v, dtype=f32),
        "ident_in": np.eye(128, dtype=ml_dtypes.bfloat16),
        "tri_in": np.triu(np.ones((128, 128))).astype(ml_dtypes.bfloat16),
        "iota_in": np.broadcast_to(np.arange(NCL, dtype=f32), (128, NCL)).copy(),
        "onec_in": np.ones((128, 1), dtype=ml_dtypes.bfloat16),
        "oner_in": np.ones((65, 128), dtype=ml_dtypes.bfloat16),
        "onef_in": np.ones((128, 128), dtype=ml_dtypes.bfloat16),
    }
    x = np.asarray(L0_attn_a_x, dtype=f32)
    in_maps = []
    for b in range(N_CORES):
        m = dict(shared)
        m["x0"] = np.ascontiguousarray(x[b].T)
        in_maps.append(m)
    return in_maps


def kernel(**inputs):
    D, S = D_FULL, S_FULL
    in_maps = _prep_inputs(D=D, S=S, **inputs)
    nc = build(D, S)
    res = run_bass_kernel_spmd(nc, in_maps, core_ids=list(range(N_CORES)))
    outs = [np.ascontiguousarray(np.asarray(res.results[b]["out"]).T)
            for b in range(N_CORES)]
    return np.stack(outs, axis=0)
